# version 19
# baseline (speedup 1.0000x reference)
import numpy as np
import ml_dtypes
import concourse.bass as bass
import concourse.mybir as mybir
from concourse.bass_utils import run_bass_kernel_spmd
from contextlib import ExitStack

F32 = mybir.dt.float32
BF16 = mybir.dt.bfloat16
AF = mybir.ActivationFunctionType
ALU = mybir.AluOpType
AX = mybir.AxisListType
bfd = ml_dtypes.bfloat16

D = 2048
TOK = 2048
NG = 4
H, DK, DV, M = 16, 64, 128, 64
FH = 5120
EPS = 1e-5
CLAMP = 32.0

UB_COLS = 36864
H2O = 0
Y2O = 0
QTO = 8192
KTO = 12288
EO = 16384
VO = 20480
GO = 28672


class Build:
    ENGINES = ("sync", "gpsimd", "scalar", "vector", "tensor")

    def __init__(self):
        self.q = {e: [] for e in self.ENGINES}
        self.ct = {}

    def add_sem(self, name, handle):
        self.ct[name] = [handle, 0]

    def op(self, eng, fn, inc=None):
        if inc is not None:
            cname, n = inc
            self.ct[cname][1] += n
            newv = self.ct[cname][1]
            h = self.ct[cname][0]

            def run(e, fn=fn, h=h, n=n):
                fn(e).then_inc(h, n)
            self.q[eng].append(run)
            return newv
        self.q[eng].append(lambda e, fn=fn: fn(e))
        return None

    def wait(self, eng, cname, val=None):
        h, n = self.ct[cname]
        v = n if val is None else val
        assert v <= n, f"wait on future: {eng} waits {cname}>={v} but emitted {n}"
        if v <= 0:
            return
        self.q[eng].append(lambda e, h=h, v=v: e.wait_ge(h, v))

    def cur(self, cname):
        return self.ct[cname][1]


def build_program(dbg=False):
    nc = bass.Bass("TRN2", target_bir_lowering=False, debug=False)

    ap = lambda t: t.ap()
    xo = ap(nc.dram_tensor("xo", [TOK, D], F32, kind="ExternalInput"))
    xp = ap(nc.dram_tensor("xp", [TOK, D], F32, kind="ExternalInput"))
    wq = ap(nc.dram_tensor("wq", [D, 1024], F32, kind="ExternalInput"))
    wk = ap(nc.dram_tensor("wk", [D, 1024], F32, kind="ExternalInput"))
    wv = ap(nc.dram_tensor("wv", [D, 2048], F32, kind="ExternalInput"))
    wg = ap(nc.dram_tensor("wg", [D, 2048], F32, kind="ExternalInput"))
    ws = ap(nc.dram_tensor("ws", [D, 1024], F32, kind="ExternalInput"))
    wo = ap(nc.dram_tensor("wo", [2048, D], F32, kind="ExternalInput"))
    gnw = ap(nc.dram_tensor("gnw", [128, 1], F32, kind="ExternalInput"))
    wfc = ap(nc.dram_tensor("wfc", [FH, D], F32, kind="ExternalInput"))
    wpj = ap(nc.dram_tensor("wpj", [D, FH], F32, kind="ExternalInput"))
    ident_in = ap(nc.dram_tensor("ident", [128, 128], BF16, kind="ExternalInput"))
    triu_in = ap(nc.dram_tensor("triu", [128, 128], BF16, kind="ExternalInput"))
    smask_in = ap(nc.dram_tensor("smask", [128, 1], F32, kind="ExternalInput"))
    out = ap(nc.dram_tensor("out", [TOK, D], F32, kind="ExternalOutput"))
    tfcT = ap(nc.dram_tensor("tfcT", [40, 128, 2048], BF16, kind="Internal"))
    tpj = ap(nc.dram_tensor("tpj", [D, FH], BF16, kind="Internal"))
    tpjT = ap(nc.dram_tensor("tpjT", [FH, D], BF16, kind="Internal"))
    if dbg:
        d_sc = ap(nc.dram_tensor("d_sc", [128, 8], F32, kind="ExternalOutput"))
        d_tfc = ap(nc.dram_tensor("d_tfc", [128, 2048], BF16, kind="ExternalOutput"))
        d_tpj = ap(nc.dram_tensor("d_tpj", [128, 2048], BF16, kind="ExternalOutput"))
        d_x1 = ap(nc.dram_tensor("d_x1", [128, 2048], F32, kind="ExternalOutput"))
        d_y1t = ap(nc.dram_tensor("d_y1t", [128, 2048], BF16, kind="ExternalOutput"))
        d_ln = ap(nc.dram_tensor("d_ln", [128, 2048], BF16, kind="ExternalOutput"))
        d_st = ap(nc.dram_tensor("d_st", [128, 8], F32, kind="ExternalOutput"))
        d_h2 = ap(nc.dram_tensor("d_h2", [128, 20480], BF16, kind="ExternalOutput"))

    b = Build()
    ctx = ExitStack()
    with ctx:
        sb = lambda name, shape, dt: ctx.enter_context(nc.sbuf_tensor(name, shape, dt))
        pst = lambda name, shape, dt: ctx.enter_context(nc.psum_tensor(name, shape, dt))

        UB = sb("UB", [128, UB_COLS], BF16)
        xg = [sb(f"xg{t}", [128, D], F32) for t in range(4)]
        wblk = [sb(f"wblk{i}", [128, 2048], BF16) for i in range(4)]
        relu_f = [sb(f"relu{i}", [128, 512], F32) for i in range(2)]
        A_P = sb("A_P", [128, 512], F32)
        B_P = sb("B_P", [128, 1024], F32)
        EP_P = sb("EP_P", [1, 1024], F32)
        A_run = sb("A_run", [128, 512], F32)
        B_run = sb("B_run", [128, 1024], F32)
        Ep_run = sb("Ep_run", [1, 1024], F32)
        Abf = [sb(f"Abf{i}", [128, 512], BF16) for i in range(2)]
        Bbf = [sb(f"Bbf{i}", [128, 1024], BF16) for i in range(2)]
        Epbf = [sb(f"Epbf{i}", [1, 1024], BF16) for i in range(2)]
        K_nat = sb("K_nat", [128, 1024], BF16)
        Et_b = sb("Et_b", [128, 1024], BF16)
        rT_b = sb("rT_b", [128, 1024], BF16)
        Ptm = [sb(f"Ptm{i}", [128, 128], BF16) for i in range(2)]
        REm = [sb(f"REm{i}", [128, 128], BF16) for i in range(2)]
        ok_sb = sb("ok_sb", [128, 1024], F32)
        Dr_sb = sb("Dr_sb", [128, 1024], F32)
        pe_bf = sb("pe_bf", [128, 1024], BF16)
        r_bf = sb("r_bf", [128, 1024], BF16)
        s_cl = sb("s_cl", [128, 512], F32)
        o_f = [sb(f"o_f{i}", [128, 512], F32) for i in range(2)]
        scrbf = sb("scrbf", [128, 2048], BF16)
        u_f = sb("u_f", [128, 512], F32)
        silu_bf = sb("silu_bf", [128, 512], BF16)
        sums = sb("sums", [128, 1], F32)
        sumsq = sb("sumsq", [128, 1], F32)
        m_t = sb("m_t", [128, 1], F32)
        msq = sb("msq", [128, 1], F32)
        ex2 = sb("ex2", [128, 1], F32)
        var_t = sb("var_t", [128, 1], F32)
        rstd = sb("rstd", [128, 1], F32)
        s16 = sb("s16", [128, 16], F32)
        rs16 = sb("rs16", [128, 16], F32)
        rstd16 = sb("rstd16", [128, 16], F32)
        ident = sb("identt", [128, 128], BF16)
        triu = sb("triut", [128, 128], BF16)
        onesc_bf = sb("onesc_bf", [128, 1], BF16)
        onesr_bf = sb("onesr_bf", [1, 128], BF16)
        onesc_f = sb("onesc_f", [128, 1], F32)
        onesr_f = sb("onesr_f", [1, 128], F32)
        onesq_f = sb("onesq_f", [128, 128], F32)
        smask = sb("smaskt", [128, 1], F32)
        gnormc = sb("gnormc", [128, 1], F32)
        asum = sb("asum", [128, 1], F32)
        asum_t = sb("asum_t", [128, 1], F32)
        asum_w = sb("asum_w", [128, 88], F32)
        am1 = sb("am1", [128, 1], F32)
        am1b = sb("am1b", [128, 1], F32)
        thr_fc = sb("thr_fc", [128, 1], F32)
        thr_pj = sb("thr_pj", [128, 1], F32)
        nthr_fc = sb("nthr_fc", [128, 1], F32)
        nthr_pj = sb("nthr_pj", [128, 1], F32)
        s_tot = sb("s_tot", [128, 1], F32)
        cvals = sb("cvals", [128, 8], F32)
        q2_bf = sb("q2_bf", [128, 2048], BF16)
        stg = sb("stgt", [128, 2048], BF16)
        twbuf = [xg[0], xg[1]]
        twbuf4 = {2: xg[2], 3: xg[3]}

        psD = pst("psD", [128, 1024], F32)
        psOK = pst("psOK", [128, 1024], F32)
        psOV = [pst(f"psOV{i}", [128, 512], F32) for i in range(2)]
        psTR = [pst(f"psTR{i}", [128, 512], BF16) for i in range(2)]
        psTile = [psD[:, 0:512], psD[:, 512:1024], psOK[:, 0:512], psOK[:, 512:1024]]

        semnames = ["T", "V", "S", "DS", "DG", "qst"]
        semnames += [f"qw{i}" for i in range(4)]
        semnames += [f"qs{i}" for i in range(16)]
        semnames += [f"qx{i}" for i in range(4)]
        semnames += [f"qb{i}" for i in range(4)]
        semnames += ["qcv0", "qcv1", "qxb0", "qxb1"]
        for name in semnames:
            b.add_sem(name, ctx.enter_context(nc.semaphore(name)))
        dma = 16

        _cvals = [EPS, -0.5, -1.0, 1.0 / 128.0, 1.0, 0.0]
        for _i, _v in enumerate(_cvals):
            nc.const_aps.register(F32, _v, cvals[:, _i:_i + 1])

        def vop(fn):
            return b.op("vector", fn, inc=("V", 1))

        def sop(fn):
            return b.op("scalar", fn, inc=("S", 1))

        def top(fn, inc=True):
            return b.op("tensor", fn, inc=("T", 1) if inc else None)

        def tnop():
            b.op("tensor", lambda e: e.nop() if hasattr(e, "nop")
                 else e.engine_nop())

        def dsop(fn, sem="DS"):
            return b.op("sync", fn, inc=(sem, dma))

        def dgop(fn, sem="DG"):
            return b.op("gpsimd", fn, inc=(sem, dma))

        ptr_free = {0: 0, 1: 0}

        def tr_batch(src_fn, n_in, emit_evac):
            bi = tr_batch.rot
            tr_batch.rot ^= 1
            pb = psTR[bi]
            b.wait("tensor", "V", ptr_free[bi])
            for j in range(n_in):
                top(lambda e, pb=pb, j=j: e.transpose(
                    pb[:, j * 128:(j + 1) * 128], src_fn(j), ident[:]),
                    inc=(j == n_in - 1))
            b.wait("vector", "T", b.cur("T"))
            ptr_free[bi] = emit_evac(pb, n_in)
        tr_batch.rot = 0

        def transpose_into(src_ap_fn, nchunks, dst_fn):
            for bb in range((nchunks + 3) // 4):
                n_in = min(4, nchunks - bb * 4)

                def evac(pb, n_in, bb=bb):
                    last = 0
                    for j in range(n_in):
                        ci = bb * 4 + j
                        last = vop(lambda e, pb=pb, j=j, ci=ci: e.tensor_copy(
                            dst_fn(ci), pb[:, j * 128:(j + 1) * 128]))
                    return last
                tr_batch(lambda j, bb=bb: src_ap_fn(bb * 4 + j), n_in, evac)

        dsop(lambda e: e.dma_start(out=ident[:], in_=ident_in[:]))
        dsop(lambda e: e.dma_start(out=triu[:], in_=triu_in[:]))
        dsop(lambda e: e.dma_start(out=smask[:], in_=smask_in[:]))
        dsop(lambda e: e.dma_start(out=gnormc[:], in_=gnw[:]))
        b.wait("vector", "DS", b.cur("DS"))
        vop(lambda e: e.memset(onesc_bf[:], 1.0))
        vop(lambda e: e.memset(onesr_bf[:], 1.0))
        vop(lambda e: e.memset(onesc_f[:], 1.0))
        vop(lambda e: e.memset(onesr_f[:], 1.0))
        vop(lambda e: e.memset(onesq_f[:], 1.0))
        vop(lambda e: e.memset(A_P[:], 0.0))
        vop(lambda e: e.memset(B_P[:], 0.0))
        vop(lambda e: e.memset(EP_P[:], 0.0))
        for _i, _v in enumerate([EPS, -0.5, -1.0, 1.0 / 128.0, 1.0, 0.0]):
            vop(lambda e, _i=_i, _v=_v: e.memset(cvals[:, _i:_i + 1], _v))

        def tern_chunks(nrows, cols):
            chunks = []
            for r in range(nrows):
                c0 = 0
                while c0 < cols:
                    cw = min(2048, cols - c0)
                    chunks.append((r, c0, cw))
                    c0 += cw
            return chunks

        def tern_stats(w_dram, nrows, cols, am_dst):
            nch = len(tern_chunks(nrows, cols))
            last_use = {i: b.cur("V") for i in range(4)}
            for i, (r, c0, cw) in enumerate(tern_chunks(nrows, cols)):
                bi = i % 4
                buf = xg[bi]
                qf, qe = (dsop, "sync") if i % 2 == 0 else (dgop, "gpsimd")
                b.wait(qe, "V", last_use[bi])
                ld = qf(lambda e, buf=buf, r=r, c0=c0, cw=cw: e.dma_start(
                    out=buf[:, 0:cw],
                    in_=w_dram[r * 128:(r + 1) * 128, c0:c0 + cw]),
                    sem=f"qx{bi}")
                b.wait("vector", f"qx{bi}", ld)
                last_use[bi] = vop(lambda e, buf=buf, cw=cw, i=i: e.reduce_sum(
                    asum_w[:, i:i + 1], buf[:, 0:cw], axis=AX.X,
                    apply_absolute_value=True))
            b.wait("vector", "V", b.cur("V"))
            vr = vop(lambda e, nch=nch: e.reduce_sum(
                asum_t[:], asum_w[:, 0:nch], axis=AX.X))
            b.wait("tensor", "V", vr)
            tm = top(lambda e: e.matmul(psOV[0][:, 0:1], onesq_f[:], asum_t[:],
                                        start=True, stop=True))
            b.wait("vector", "T", tm)
            n_el = nrows * 128 * cols
            vth = vop(lambda e, n_el=n_el, am_dst=am_dst: e.tensor_scalar(
                am_dst[:], psOV[0][:, 0:1], 1.0 / n_el, 1e-5, ALU.mult, ALU.max))
            b.wait("vector", "V", vth)

        tern_stats(wfc, 40, 2048, am1)
        vop(lambda e: e.tensor_scalar(thr_fc[:], am1[:], 0.7, None, ALU.mult))
        v0 = vop(lambda e: e.tensor_scalar(
            nthr_fc[:], am1[:], -0.7, None, ALU.mult))
        b.wait("vector", "V", v0)
        q_free_T = 0
        buf_free = {0: b.cur("V"), 1: b.cur("V")}
        pj_chunks = tern_chunks(16, FH)
        pj_free = {2: b.cur("V"), 3: b.cur("V")}
        pj_i = 0

        def emit_pj_stats_chunk():
            nonlocal pj_i
            if pj_i >= len(pj_chunks):
                return
            r, c0, cw = pj_chunks[pj_i]
            bi = 2 + pj_i % 2
            buf = twbuf4[bi]
            qf, qe = (dsop, "sync") if pj_i % 2 == 0 else (dgop, "gpsimd")
            b.wait(qe, "V", pj_free[bi])
            ld = qf(lambda e, buf=buf, r=r, c0=c0, cw=cw: e.dma_start(
                out=buf[:, 0:cw], in_=wpj[r * 128:(r + 1) * 128, c0:c0 + cw]),
                sem=f"qx{bi}")
            b.wait("vector", f"qx{bi}", ld)
            pj_free[bi] = vop(lambda e, buf=buf, cw=cw, pi=pj_i: e.reduce_sum(
                asum_w[:, 40 + pi:41 + pi], buf[:, 0:cw], axis=AX.X,
                apply_absolute_value=True))
            pj_i += 1

        for i in range(40):
            bi = i % 2
            buf = twbuf[bi]
            qf, qe = (dsop, "sync") if i % 2 == 0 else (dgop, "gpsimd")
            b.wait(qe, "V", buf_free[bi])
            ld = qf(lambda e, buf=buf, i=i: e.dma_start(
                out=buf[:], in_=wfc[i * 128:(i + 1) * 128, :]), sem=f"qx{bi}")
            emit_pj_stats_chunk()
            b.wait("vector", f"qx{bi}", ld)
            b.wait("vector", "T", q_free_T)
            vn = vop(lambda e, buf=buf: e.tensor_scalar(
                scrbf[:], buf[:], nthr_fc[:, 0:1], None, ALU.is_lt))
            b.wait("vector", "V", vn)
            buf_free[bi] = vop(lambda e, buf=buf: e.scalar_tensor_tensor(
                q2_bf[:], buf[:], thr_fc[:, 0:1], scrbf[:],
                ALU.is_gt, ALU.subtract))
            b.wait("tensor", "V", buf_free[bi])
            for bb in range(4):
                def evac(pb, n_in, bb=bb):
                    return vop(lambda e, pb=pb, bb=bb: e.tensor_copy(
                        stg[:, bb * 512:(bb + 1) * 512], pb[:]))
                tr_batch(lambda j, bb=bb: q2_bf[:, (bb * 4 + j) * 128:
                                                (bb * 4 + j + 1) * 128], 4, evac)
            q_free_T = b.cur("T")
            vs = b.cur("V")
            b.wait("sync", "V", vs)
            st = dsop(lambda e, i=i: e.dma_start(out=tfcT[i, :, :], in_=stg[:]),
                      sem="qst")
            b.wait("vector", "qst", st)
        while pj_i < len(pj_chunks):
            emit_pj_stats_chunk()
        b.wait("vector", "V", b.cur("V"))
        vr2 = vop(lambda e, n=len(pj_chunks): e.reduce_sum(
            asum_t[:], asum_w[:, 40:40 + n], axis=AX.X))
        b.wait("tensor", "V", vr2)
        tm = top(lambda e: e.matmul(psOV[0][:, 0:1], onesq_f[:], asum_t[:],
                                    start=True, stop=True))
        b.wait("vector", "T", tm)
        vth = vop(lambda e: e.tensor_scalar(
            am1b[:], psOV[0][:, 0:1], 1.0 / (2048 * FH), 1e-5, ALU.mult, ALU.max))
        b.wait("vector", "V", vth)
        vop(lambda e: e.tensor_scalar(thr_pj[:], am1b[:], 0.7, None, ALU.mult))
        vop(lambda e: e.tensor_scalar(
            nthr_pj[:], am1b[:], -0.7, None, ALU.mult))
        va = vop(lambda e: e.tensor_tensor(s_tot[:], am1[:], am1[:], ALU.mult))
        b.wait("vector", "V", va)
        vb = vop(lambda e: e.tensor_tensor(s_tot[:], s_tot[:], am1b[:], ALU.mult))
        b.wait("vector", "V", vb)
        v2 = vop(lambda e: e.tensor_scalar(s_tot[:], s_tot[:], 0.5, None, ALU.mult))
        b.wait("vector", "V", v2)
        buf_free = {0: b.cur("V"), 1: b.cur("V"), 2: b.cur("V"), 3: b.cur("V")}
        b.wait("vector", "T", q_free_T)
        stg_free = {i: b.cur("qst") for i in range(4)}
        sreg = [0]
        for i, (r, c0, cw) in enumerate(tern_chunks(16, FH)):
            bi = i % 4
            buf = xg[bi]
            qf, qe = (dsop, "sync") if i % 2 == 0 else (dgop, "gpsimd")
            b.wait(qe, "V", buf_free[bi])
            ld = qf(lambda e, buf=buf, r=r, c0=c0, cw=cw: e.dma_start(
                out=buf[:, 0:cw], in_=wpj[r * 128:(r + 1) * 128, c0:c0 + cw]),
                sem=f"qx{bi}")
            b.wait("vector", f"qx{bi}", ld)
            vn = vop(lambda e, buf=buf, cw=cw: e.tensor_scalar(
                scrbf[:, 0:cw], buf[:, 0:cw], nthr_pj[:, 0:1], None, ALU.is_lt))
            b.wait("vector", "V", vn)
            buf_free[bi] = vop(lambda e, buf=buf, cw=cw: e.scalar_tensor_tensor(
                q2_bf[:, 0:cw], buf[:, 0:cw], thr_pj[:, 0:1], scrbf[:, 0:cw],
                ALU.is_gt, ALU.subtract))
            b.wait("tensor", "V", buf_free[bi])
            for bb in range(cw // 512):
                reg = sreg[0]
                sreg[0] = (sreg[0] + 1) % 4

                def evac(pb, n_in, reg=reg, r=r, c0=c0, bb=bb):
                    b.wait("vector", "qst", stg_free[reg])
                    last = 0
                    for j in range(4):
                        last = vop(lambda e, pb=pb, j=j, reg=reg: e.tensor_copy(
                            stg[:, reg * 512 + j * 128:reg * 512 + (j + 1) * 128],
                            pb[:, j * 128:(j + 1) * 128]))
                    b.wait("sync", "V", last)
                    stg_free[reg] = dsop(
                        lambda e, reg=reg, r=r, c0=c0, bb=bb: e.dma_start(
                            out=tpjT[c0 + bb * 512:c0 + (bb + 1) * 512,
                                     r * 128:(r + 1) * 128].rearrange(
                                "(j p) c -> p j c", j=4),
                            in_=stg[:, reg * 512:(reg + 1) * 512].rearrange(
                                "p (j c) -> p j c", j=4)), sem="qst")
                    return last
                tr_batch(lambda j, bb=bb: q2_bf[:, bb * 512 + j * 128:
                                                bb * 512 + (j + 1) * 128],
                         4, evac)
        b.wait("sync", "qst", b.cur("qst"))
        b.wait("gpsimd", "qst", b.cur("qst"))

        def y1T_ap(dc):
            return UB[:, VO + dc * 512:VO + (dc + 1) * 512]

        def y2T_ap(dc):
            return UB[:, Y2O + dc * 512:Y2O + (dc + 1) * 512]

        def y2T_tile_ap(dc, t):
            c = Y2O + dc * 512 + t * 128
            return UB[:, c:c + 128]

        def y3T_ap(dc):
            v = UB[:, GO:GO + 8192].rearrange("p (t c) -> p t c", t=4)
            return v[:, :, dc * 128:(dc + 1) * 128]

        def E_ap(t, h=None):
            c = EO + t * 1024
            if h is None:
                return UB[:, c:c + 1024]
            return UB[:, c + h * 64:c + (h + 1) * 64]

        def V_ap(t, h):
            c = VO + t * 2048 + h * 128
            return UB[:, c:c + 128]

        def G_ap(t, sl):
            c = GO + t * 2048 + sl * 512
            return UB[:, c:c + 512]

        def qt_ap(h, t):
            p0 = 64 * (h % 2)
            c = QTO + (h // 2) * 512 + t * 128
            return UB[p0:p0 + 64, c:c + 128]

        def kt_ap(h, t):
            p0 = 64 * (h % 2)
            c = KTO + (h // 2) * 512 + t * 128
            return UB[p0:p0 + 64, c:c + 128]

        def gT_ap(vc, t):
            off = QTO if vc < 8 else KTO
            c = off + (vc % 8) * 512 + t * 128
            return UB[:, c:c + 128]

        def a_blk(tn, h):
            p0 = 64 * (h % 2)
            c = (h // 2) * 64
            return tn[p0:p0 + 64, c:c + 64]

        def b_blk(tn, h):
            p0 = 64 * (h % 2)
            c = (h // 2) * 128
            return tn[p0:p0 + 64, c:c + 128]

        def ln_tile(x_t, dst_off, t, tmajor=False, dbg_tap=False):
            b.wait("vector", "S", b.cur("S"))
            vop(lambda e, x_t=x_t: e.reduce_sum(sums[:], x_t[:], axis=AX.X))
            vs = vop(lambda e, x_t=x_t: e.scalar_tensor_tensor(
                scrbf[:, 0:2048], x_t[:], 1.0, x_t[:], ALU.mult, ALU.mult,
                accum_out=sumsq[:]))
            b.wait("vector", "V", vs)
            vm = vop(lambda e: e.tensor_scalar(m_t[:], sums[:], 1.0 / D, None,
                                               ALU.mult))
            b.wait("vector", "V", vm)
            vop(lambda e: e.tensor_tensor(msq[:], m_t[:], m_t[:], ALU.mult))
            vx = vop(lambda e: e.tensor_scalar(ex2[:], sumsq[:], 1.0 / D, None,
                                               ALU.mult))
            b.wait("vector", "V", vx)
            vv = vop(lambda e: e.tensor_tensor(var_t[:], ex2[:], msq[:], ALU.subtract))
            b.wait("scalar", "V", vv)
            sl_ = sop(lambda e: e.activation(out=rstd[:], in_=var_t[:], func=AF.Ln,
                                             bias=EPS))
            b.wait("scalar", "S", sl_)
            sr = sop(lambda e: e.activation(out=rstd[:], in_=rstd[:], func=AF.Exp,
                                            scale=-0.5))
            b.wait("vector", "S", sr)
            vy = vop(lambda e, x_t=x_t: e.tensor_scalar(
                scrbf[:, 0:2048], x_t[:], m_t[:, 0:1], rstd[:, 0:1],
                ALU.subtract, ALU.mult))
            if dbg_tap:
                vop(lambda e: e.tensor_copy(s_cl[:, 0:1], sums[:]))
                vop(lambda e: e.tensor_copy(s_cl[:, 1:2], sumsq[:]))
                vop(lambda e: e.tensor_copy(s_cl[:, 2:3], m_t[:]))
                vop(lambda e: e.tensor_copy(s_cl[:, 3:4], msq[:]))
                vop(lambda e: e.tensor_copy(s_cl[:, 4:5], ex2[:]))
                vop(lambda e: e.tensor_copy(s_cl[:, 5:6], var_t[:]))
                vd = vop(lambda e: e.tensor_copy(s_cl[:, 6:7], rstd[:]))
                b.wait("sync", "V", vd)
                dsop(lambda e: e.dma_start(out=d_st[:], in_=s_cl[:, 0:8]))
                st_ = dsop(lambda e: e.dma_start(out=d_ln[:], in_=scrbf[:]))
                for eng in ("vector", "tensor", "scalar", "gpsimd"):
                    b.wait(eng, "DS", st_)
            b.wait("tensor", "V", vy)
            for bb in range(4):
                def evac(pb, n_in, bb=bb):
                    last = 0
                    for j in range(4):
                        dc = bb * 4 + j
                        if tmajor:
                            col = dst_off + t * 2048 + dc * 128
                        else:
                            col = dst_off + dc * 512 + t * 128
                        last = vop(lambda e, pb=pb, j=j, col=col: e.tensor_copy(
                            UB[:, col:col + 128], pb[:, j * 128:(j + 1) * 128]))
                    return last
                tr_batch(lambda j, bb=bb: scrbf[:, (bb * 4 + j) * 128:
                                                (bb * 4 + j + 1) * 128], 4, evac)

        def ffn(ysrc_ap_fn):
            b.wait("vector", "T", b.cur("T"))
            b.wait("vector", "S", b.cur("S"))
            b.wait("tensor", "V", b.cur("V"))
            b.wait("tensor", "S", b.cur("S"))
            wfree = {i: b.cur("T") for i in range(4)}
            sfree = {0: 0, 1: 0}
            relu_s = {0: 0, 1: 0}
            for fc in range(40):
                bi = fc % 4
                buf = wblk[bi]
                pi = fc % 2
                b.wait("sync", "T", wfree[bi])
                ld = dsop(lambda e, buf=buf, fc=fc: e.dma_start(
                    out=buf[:], in_=tfcT[fc, :, :]), sem=f"qw{bi}")
                pb = psOV[pi]
                b.wait("tensor", f"qw{bi}", ld)
                b.wait("tensor", "S", relu_s[pi])
                for dc in range(16):
                    top(lambda e, pb=pb, buf=buf, dc=dc: e.matmul(
                        pb[:], buf[:, dc * 128:(dc + 1) * 128], ysrc_ap_fn(dc),
                        start=(dc == 0), stop=(dc == 15)), inc=(dc == 15))
                wfree[bi] = b.cur("T")
                b.wait("scalar", "T", wfree[bi])
                b.wait("scalar", "V", sfree[pi])
                relu_s[pi] = sop(lambda e, pb=pb, pi=pi: e.activation(
                    out=relu_f[pi][:], in_=pb[:], func=AF.Relu))
                b.wait("vector", "S", relu_s[pi])
                sfree[pi] = vop(lambda e, pi=pi, fc=fc: e.scalar_tensor_tensor(
                    UB[:, H2O + fc * 512:H2O + (fc + 1) * 512],
                    relu_f[pi][:], 1.0, relu_f[pi][:], ALU.mult, ALU.mult))
            pw_free = {i: b.cur("T") for i in range(16)}
            for dg in range(4):
                b.wait("tensor", "V", b.cur("V"))
                for fc in range(40):
                    bi = (dg * 40 + fc) % 16
                    wsl = wblk[bi // 4][:, (bi % 4) * 512:(bi % 4 + 1) * 512]
                    b.wait("sync", "T", pw_free[bi])
                    ld = dsop(lambda e, wsl=wsl, dg=dg, fc=fc: e.dma_start(
                        out=wsl, in_=tpjT[fc * 128:(fc + 1) * 128,
                                          dg * 512:(dg + 1) * 512]),
                        sem=f"qs{bi}")
                    b.wait("tensor", f"qs{bi}", ld)
                    for t in range(4):
                        top(lambda e, t=t, wsl=wsl, fc=fc: e.matmul(
                            psTile[t],
                            UB[:, H2O + fc * 512 + t * 128:
                               H2O + fc * 512 + (t + 1) * 128],
                            wsl, start=(fc == 0), stop=(fc == 39)),
                            inc=(t == 3))
                    pw_free[bi] = b.cur("T")
                b.wait("vector", "T", b.cur("T"))
                for t in range(4):
                    vop(lambda e, t=t, dg=dg: e.scalar_tensor_tensor(
                        xg[t][:, dg * 512:(dg + 1) * 512], psTile[t],
                        s_tot[:, 0:1], xg[t][:, dg * 512:(dg + 1) * 512],
                        ALU.mult, ALU.add))

        def proj_transposed(w_dram, dst_off, scale):
            b.wait("tensor", "V", b.cur("V"))
            b.wait("tensor", "S", b.cur("S"))
            evac = {0: b.cur("V"), 1: b.cur("V")}
            for half in range(2):
                b.wait("gpsimd", "T", b.cur("T"))
                lds = []
                for dc in range(16):
                    lds.append(dgop(lambda e, dc=dc, half=half, w=w_dram: e.dma_start(
                        out=UB[:, VO + dc * 512:VO + (dc + 1) * 512],
                        in_=w[dc * 128:(dc + 1) * 128,
                              half * 512:(half + 1) * 512]),
                        sem=f"qb{dc // 4}"))
                for kci in range(4):
                    kc = half * 4 + kci
                    pb = psOV[kci % 2]
                    b.wait("tensor", "V", evac[kci % 2])
                    for dc in range(16):
                        if kci == 0 and dc % 4 == 0:
                            b.wait("tensor", f"qb{dc // 4}", lds[dc + 3])
                        top(lambda e, pb=pb, dc=dc, kci=kci: e.matmul(
                            pb[:],
                            UB[:, VO + dc * 512 + kci * 128:
                               VO + dc * 512 + (kci + 1) * 128],
                            y2T_ap(dc), start=(dc == 0), stop=(dc == 15)),
                            inc=(dc == 15))
                    b.wait("vector", "T", b.cur("T"))
                    evac[kci % 2] = vop(lambda e, pb=pb, kc=kc: e.tensor_scalar(
                        UB[:, dst_off + kc * 512:dst_off + (kc + 1) * 512],
                        pb[:], scale, None, ALU.mult))

        def proj_natural(w_dram, featdim, evac_fn):
            nvg = featdim // 512
            wfree = {i: b.cur("T") for i in range(16)}
            for vg in range(nvg):
                b.wait("tensor", "V", b.cur("V"))
                b.wait("tensor", "S", b.cur("S"))
                for dc in range(16):
                    bi = (vg * 16 + dc) % 16
                    wsl = wblk[bi // 4][:, (bi % 4) * 512:(bi % 4 + 1) * 512]
                    b.wait("gpsimd", "T", wfree[bi])
                    ld = dgop(lambda e, wsl=wsl, dc=dc, vg=vg, w=w_dram: e.dma_start(
                        out=wsl, in_=w[dc * 128:(dc + 1) * 128,
                                       vg * 512:(vg + 1) * 512]),
                        sem=f"qs{bi}")
                    b.wait("tensor", f"qs{bi}", ld)
                    for t in range(4):
                        top(lambda e, t=t, wsl=wsl, dc=dc: e.matmul(
                            psTile[t], y2T_tile_ap(dc, t), wsl,
                            start=(dc == 0), stop=(dc == 15)), inc=(t == 3))
                    wfree[bi] = b.cur("T")
                tl = b.cur("T")
                for t in range(4):
                    evac_fn(t, vg, tl)

        def state_updates(t, own, cidx):
            Arun = A_run if own else A_P
            Brun = B_run if own else B_P
            Eprun = Ep_run if own else EP_P
            b.wait("tensor", "V", b.cur("V"))
            for h in range(16):
                top(lambda e, h=h, t=t: e.matmul(
                    a_blk(psD, h), K_nat[:, h * 64:(h + 1) * 64], E_ap(t, h),
                    start=True, stop=True), inc=False)
            for h in range(16):
                top(lambda e, h=h, t=t: e.matmul(
                    b_blk(psOK, h), E_ap(t, h), V_ap(t, h),
                    start=True, stop=True), inc=False)
            top(lambda e, t=t: e.matmul(psOV[0][0:1, :], onesc_bf[:],
                                        E_ap(t)[:, 0:512], start=True, stop=True))
            top(lambda e, t=t: e.matmul(psOV[1][0:1, :], onesc_bf[:],
                                        E_ap(t)[:, 512:1024], start=True, stop=True),
                inc=True)
            b.wait("vector", "T", b.cur("T"))
            vop(lambda e, A=Arun: e.tensor_tensor(
                A[:], A[:], psD[:, 0:512], ALU.add))
            vop(lambda e, B=Brun: e.tensor_tensor(
                B[:], B[:], psOK[:], ALU.add))
            vop(lambda e, Ep=Eprun: e.tensor_tensor(
                Ep[:, 0:512], Ep[:, 0:512], psOV[0][0:1, :], ALU.add))
            vop(lambda e, Ep=Eprun: e.tensor_tensor(
                Ep[:, 512:1024], Ep[:, 512:1024], psOV[1][0:1, :], ALU.add))
            if own:
                b.wait("vector", "V", b.cur("V"))
                nb = (cidx + 1) % 2
                vop(lambda e, nb=nb: e.tensor_copy(Abf[nb][:], A_run[:]))
                vop(lambda e, nb=nb: e.tensor_copy(Bbf[nb][:], B_run[:]))
                vop(lambda e, nb=nb: e.tensor_copy(Epbf[nb][:], Ep_run[:]))

        def scan_own(g, t):
            cidx = 4 * g + t
            cb = cidx % 2
            transpose_into(lambda ci, t=t: UB[:, KTO + ci * 512 + t * 128:
                                              KTO + ci * 512 + t * 128 + 128],
                           8, lambda ci: K_nat[:, ci * 128:(ci + 1) * 128])
            transpose_into(lambda ci, t=t: UB[:, EO + t * 1024 + ci * 128:
                                              EO + t * 1024 + (ci + 1) * 128],
                           8, lambda ci: Et_b[:, ci * 128:(ci + 1) * 128])
            b.wait("tensor", "V", b.cur("V"))
            b.wait("tensor", "S", b.cur("S"))
            pt_order = list(range(0, 16, 2)) + list(range(1, 16, 2))
            for j, h in enumerate(pt_order):
                slot = psD[:, (h % 8) * 128:(h % 8 + 1) * 128] if h < 8 \
                    else psOK[:, (h % 8) * 128:(h % 8 + 1) * 128]
                if j == 8:
                    tnop()
                top(lambda e, slot=slot, h=h, t=t: e.matmul(
                    slot, kt_ap(h, t), qt_ap(h, t), start=True, stop=True),
                    inc=(j == 15))
            tptb = b.cur("T")
            b.wait("vector", "T", tptb)
            b.wait("vector", "S", b.cur("S"))
            for h in range(16):
                slot = psD[:, (h % 8) * 128:(h % 8 + 1) * 128] if h < 8 \
                    else psOK[:, (h % 8) * 128:(h % 8 + 1) * 128]
                vop(lambda e, slot=slot, h=h: e.tensor_tensor(
                    scrbf[:, h * 128:(h + 1) * 128], slot, triu[:], ALU.mult))
            vptm = b.cur("V")
            b.wait("tensor", "V", vptm)
            for sl in range(2):
                top(lambda e, sl=sl, t=t: e.matmul(
                    psD[:, sl * 512:(sl + 1) * 512], triu[:],
                    E_ap(t)[:, sl * 512:(sl + 1) * 512], start=True, stop=False))
                top(lambda e, sl=sl, cb=cb: e.matmul(
                    psD[:, sl * 512:(sl + 1) * 512], onesr_bf[:],
                    Epbf[cb][:, sl * 512:(sl + 1) * 512], start=False, stop=True))
            for j, h in enumerate(list(range(0, 16, 2)) + list(range(1, 16, 2))):
                top(lambda e, h=h, t=t: e.matmul(
                    psOK[:, h * 64:(h + 1) * 64],
                    scrbf[:, h * 128:(h + 1) * 128], E_ap(t, h),
                    start=True, stop=False))
                top(lambda e, h=h, t=t, cb=cb: e.matmul(
                    psOK[:, h * 64:(h + 1) * 64], qt_ap(h, t),
                    a_blk(Abf[cb], h), start=False, stop=True),
                    inc=(j == 15))
            b.wait("vector", "T", b.cur("T"))
            vrd = vop(lambda e: e.reciprocal(Dr_sb[:], psD[:]))
            b.wait("vector", "V", vrd)
            vok = vop(lambda e: e.tensor_tensor(ok_sb[:], psOK[:], Dr_sb[:], ALU.mult))
            b.wait("tensor", "V", vok)
            state_updates(t, True, cidx)
            b.wait("scalar", "V", vok)
            spe = sop(lambda e: e.activation(out=pe_bf[:], in_=ok_sb[:], func=AF.Exp))
            b.wait("vector", "S", spe)
            v16 = vop(lambda e: e.reduce_sum(
                s16[:], pe_bf[:].rearrange("p (h m) -> p h m", m=64), axis=AX.X))
            b.wait("vector", "V", v16)
            vr16 = vop(lambda e: e.reciprocal(rs16[:], s16[:]))
            b.wait("vector", "V", vr16)
            for h in range(16):
                vop(lambda e, h=h: e.scalar_tensor_tensor(
                    r_bf[:, h * 64:(h + 1) * 64], pe_bf[:, h * 64:(h + 1) * 64],
                    rs16[:, h:h + 1], Dr_sb[:, h * 64:(h + 1) * 64],
                    ALU.mult, ALU.mult))
            vrdone = b.cur("V")
            b.wait("tensor", "V", vrdone)
            transpose_into(lambda ci: r_bf[:, ci * 128:(ci + 1) * 128], 8,
                           lambda ci: rT_b[:, ci * 128:(ci + 1) * 128])
            b.wait("tensor", "V", b.cur("V"))
            for j, h in enumerate(list(range(0, 16, 2)) + list(range(1, 16, 2))):
                slot = psD[:, (h % 8) * 128:(h % 8 + 1) * 128] if h < 8 \
                    else psOK[:, (h % 8) * 128:(h % 8 + 1) * 128]
                if j == 8:
                    tnop()
                top(lambda e, slot=slot, h=h: e.matmul(
                    slot,
                    Et_b[64 * (h % 2):64 * (h % 2) + 64,
                         (h // 2) * 128:(h // 2 + 1) * 128],
                    rT_b[64 * (h % 2):64 * (h % 2) + 64,
                         (h // 2) * 128:(h // 2 + 1) * 128],
                    start=True, stop=True), inc=(j == 15))
            trtb = b.cur("T")
            b.wait("vector", "T", trtb)
            b.wait("vector", "S", b.cur("S"))
            for h in range(16):
                slot = psD[:, (h % 8) * 128:(h % 8 + 1) * 128] if h < 8 \
                    else psOK[:, (h % 8) * 128:(h % 8 + 1) * 128]
                vop(lambda e, slot=slot, h=h: e.tensor_tensor(
                    scrbf[:, h * 128:(h + 1) * 128], slot, triu[:], ALU.mult))
            vrem = b.cur("V")
            b.wait("tensor", "V", vrem)
            oev = {0: b.cur("V"), 1: b.cur("V")}
            for hp in range(2):
                heads = list(range(8 * hp, 8 * hp + 8))
                b.wait("tensor", "V", oev[0])
                b.wait("tensor", "V", oev[1])
                for j, h in enumerate([x for x in heads if x % 2 == 0]
                                      + [x for x in heads if x % 2 == 1]):
                    pov = psOV[(h // 4) % 2]
                    top(lambda e, pov=pov, h=h, t=t: e.matmul(
                        pov[:, (h % 4) * 128:(h % 4 + 1) * 128],
                        scrbf[:, h * 128:(h + 1) * 128],
                        V_ap(t, h), start=True, stop=False))
                    top(lambda e, pov=pov, h=h, cb=cb: e.matmul(
                        pov[:, (h % 4) * 128:(h % 4 + 1) * 128],
                        rT_b[64 * (h % 2):64 * (h % 2) + 64,
                             (h // 2) * 128:(h // 2 + 1) * 128],
                        b_blk(Bbf[cb], h), start=False, stop=True),
                        inc=(j == 7))
                tmark = b.cur("T")
                b.wait("vector", "T", tmark)
                for sl in (2 * hp, 2 * hp + 1):
                    pov_s = psOV[sl % 2]
                    ob = o_f[sl % 2]
                    oev[sl % 2] = vop(lambda e, ob=ob, pov_s=pov_s: e.tensor_copy(
                        ob[:], pov_s[:]))
                    b.wait("vector", "V", oev[sl % 2])
                    vsq = vop(lambda e, ob=ob, sl=sl: e.scalar_tensor_tensor(
                        scrbf[:, sl * 512:(sl + 1) * 512], ob[:], 1.0, ob[:],
                        ALU.mult, ALU.mult))
                    b.wait("vector", "V", vsq)
                    v2_ = vop(lambda e, sl=sl: e.reduce_sum(
                        s16[:, sl * 4:(sl + 1) * 4],
                        scrbf[:, sl * 512:(sl + 1) * 512].rearrange(
                            "p (hh v) -> p hh v", v=128), axis=AX.X))
                    b.wait("scalar", "V", v2_)
                    sln = sop(lambda e, sl=sl: e.activation(
                        out=rstd16[:, sl * 4:(sl + 1) * 4],
                        in_=s16[:, sl * 4:(sl + 1) * 4], func=AF.Ln,
                        scale=1.0 / 128.0, bias=EPS))
                    b.wait("scalar", "S", sln)
                    sr_ = sop(lambda e, sl=sl: e.activation(
                        out=rstd16[:, sl * 4:(sl + 1) * 4],
                        in_=rstd16[:, sl * 4:(sl + 1) * 4], func=AF.Exp,
                        scale=-0.5))
                    seg = sop(lambda e, t=t, sl=sl: e.activation(
                        out=scrbf[:, sl * 512:(sl + 1) * 512], in_=G_ap(t, sl),
                        func=AF.Exp, scale=-1.0))
                    b.wait("vector", "S", seg)
                    vu = vop(lambda e, sl=sl: e.tensor_scalar(
                        u_f[:], scrbf[:, sl * 512:(sl + 1) * 512], 1.0, None,
                        ALU.add))
                    b.wait("vector", "V", vu)
                    vrec = vop(lambda e: e.reciprocal(u_f[:], u_f[:]))
                    b.wait("vector", "V", vrec)
                    vsil = vop(lambda e, t=t, sl=sl: e.tensor_tensor(
                        silu_bf[:], G_ap(t, sl), u_f[:], ALU.mult))
                    b.wait("vector", "S", sr_)
                    b.wait("vector", "V", vsil)
                    for hh in range(4):
                        hx = sl * 4 + hh
                        vop(lambda e, ob=ob, hx=hx, hh=hh, sl=sl:
                            e.scalar_tensor_tensor(
                            scrbf[:, sl * 512 + hh * 128:
                                  sl * 512 + (hh + 1) * 128],
                            ob[:, hh * 128:(hh + 1) * 128], rstd16[:, hx:hx + 1],
                            silu_bf[:, hh * 128:(hh + 1) * 128],
                            ALU.mult, ALU.mult))
            vgate = b.cur("V")
            b.wait("tensor", "V", vgate)
            transpose_into(lambda ci: scrbf[:, ci * 128:(ci + 1) * 128], 16,
                           lambda ci, t=t: gT_ap(ci, t))

        def wo_phase(g):
            wfree = {i: b.cur("T") for i in range(16)}
            for dg in range(4):
                b.wait("tensor", "V", b.cur("V"))
                for vc in range(16):
                    bi = (dg * 16 + vc) % 16
                    wsl = wblk[bi // 4][:, (bi % 4) * 512:(bi % 4 + 1) * 512]
                    b.wait("gpsimd", "T", wfree[bi])
                    ld = dgop(lambda e, wsl=wsl, vc=vc, dg=dg: e.dma_start(
                        out=wsl, in_=wo[vc * 128:(vc + 1) * 128,
                                        dg * 512:(dg + 1) * 512]),
                        sem=f"qs{bi}")
                    b.wait("vector", f"qs{bi}", ld)
                    vf = vop(lambda e, wsl=wsl: e.tensor_scalar(
                        wsl, wsl, gnormc[:, 0:1], None, ALU.mult))
                    b.wait("tensor", "V", vf)
                    for t in range(4):
                        top(lambda e, t=t, wsl=wsl, vc=vc: e.matmul(
                            psTile[t], gT_ap(vc, t), wsl,
                            start=(vc == 0), stop=(vc == 15)), inc=(t == 3))
                    wfree[bi] = b.cur("T")
                b.wait("vector", "T", b.cur("T"))
                for t in range(4):
                    vop(lambda e, t=t, dg=dg: e.scalar_tensor_tensor(
                        xg[t][:, dg * 512:(dg + 1) * 512], psTile[t], 1.0,
                        xg[t][:, dg * 512:(dg + 1) * 512], ALU.mult, ALU.add))

        def group(src_dram, g, own):
            b.wait("sync", "V", b.cur("V"))
            b.wait("sync", "T", b.cur("T"))
            for t in range(4):
                ld = dsop(lambda e, t=t, g=g, src=src_dram: e.dma_start(
                    out=xg[t][:], in_=src[g * 512 + t * 128:
                                          g * 512 + (t + 1) * 128, :]),
                    sem=f"qx{t}")
                b.wait("vector", f"qx{t}", ld)
                tap = dbg and (not dbg_state["done"]) and t == 0 and g == 0
                ln_tile(xg[t], VO, t, dbg_tap=tap)
            if dbg and not dbg_state["done"]:
                vv = b.cur("V")
                b.wait("sync", "V", vv)
                st = dsop(lambda e: e.dma_start(out=d_y1t[:, 0:512],
                                                in_=UB[:, VO:VO + 512]))
                for eng in ("vector", "tensor", "scalar", "gpsimd"):
                    b.wait(eng, "DS", st)
            ffn(y1T_ap)
            if dbg and not dbg_state["done"]:
                dbg_state["done"] = True
                vv = b.cur("V")
                b.wait("sync", "V", vv)
                b.wait("sync", "T", b.cur("T"))
                dsop(lambda e: e.dma_start(out=d_x1[:], in_=xg[0][:]))
                st = dsop(lambda e: e.dma_start(out=d_h2[:],
                                                in_=UB[:, H2O:H2O + 20480]))
                for eng in ("vector", "tensor", "scalar", "gpsimd"):
                    b.wait(eng, "DS", st)
            for t in range(4):
                ln_tile(xg[t], Y2O, t)
            if own:
                proj_transposed(wq, QTO, DK ** -0.5)
            proj_transposed(wk, KTO, 1.0)
            s_exp = {"s": 0}

            def s_evac(t, vg, tmark):
                b.wait("vector", "T", tmark)
                b.wait("vector", "S", s_exp["s"])
                vc_ = vop(lambda e, t=t, vg=vg: e.tensor_scalar(
                    s_cl[:], psTile[t], CLAMP, -CLAMP, ALU.min, ALU.max))
                b.wait("scalar", "V", vc_)
                s_exp["s"] = sop(lambda e, t=t, vg=vg: e.activation(
                    out=E_ap(t)[:, vg * 512:(vg + 1) * 512], in_=s_cl[:],
                    func=AF.Exp))
            proj_natural(ws, 1024, s_evac)

            def v_evac(t, vg, tmark):
                b.wait("vector", "T", tmark)
                vop(lambda e, t=t, vg=vg: e.tensor_copy(
                    UB[:, VO + t * 2048 + vg * 512:VO + t * 2048 + (vg + 1) * 512],
                    psTile[t]))
            proj_natural(wv, 2048, v_evac)
            if own:
                def g_evac(t, vg, tmark):
                    b.wait("vector", "T", tmark)
                    vop(lambda e, t=t, vg=vg: e.tensor_copy(
                        G_ap(t, vg), psTile[t]))
                proj_natural(wg, 2048, g_evac)
            b.wait("tensor", "S", b.cur("S"))
            for t in range(4):
                if own:
                    scan_own(g, t)
                else:
                    transpose_into(
                        lambda ci, t=t: UB[:, KTO + ci * 512 + t * 128:
                                           KTO + ci * 512 + t * 128 + 128],
                        8, lambda ci: K_nat[:, ci * 128:(ci + 1) * 128])
                    state_updates(t, False, 0)
            if not own:
                return
            wo_phase(g)
            b.wait("vector", "S", b.cur("S"))
            for t in range(4):
                ln_tile(xg[t], GO, t, tmajor=True)
            ffn(y3T_ap)
            vend = b.cur("V")
            b.wait("sync", "V", vend)
            for t in range(4):
                dsop(lambda e, t=t, g=g: e.dma_start(
                    out=out[g * 512 + t * 128:g * 512 + (t + 1) * 128, :],
                    in_=xg[t][:]))

        def dbg_dump():
            vop(lambda e: e.memset(s_cl[:, 0:8], 0.0))
            vop(lambda e: e.tensor_copy(s_cl[:, 0:1], thr_fc[:]))
            vop(lambda e: e.tensor_copy(s_cl[:, 1:2], thr_pj[:]))
            vop(lambda e: e.tensor_copy(s_cl[:, 2:3], s_tot[:]))
            vop(lambda e: e.tensor_copy(s_cl[:, 3:4], am1[:]))
            vop(lambda e: e.tensor_copy(s_cl[:, 4:5], sums[:]))
            vop(lambda e: e.tensor_copy(s_cl[:, 5:6], sumsq[:]))
            vop(lambda e: e.tensor_copy(s_cl[:, 6:7], var_t[:]))
            vd = vop(lambda e: e.tensor_copy(s_cl[:, 7:8], rstd[:]))
            b.wait("sync", "V", vd)
            dsop(lambda e: e.dma_start(out=d_sc[:], in_=s_cl[:, 0:8]))
            dsop(lambda e: e.dma_start(out=d_tfc[:], in_=tfcT[39, :, :]))
            st = dsop(lambda e: e.dma_start(out=d_tpj[:], in_=tfcT[38, :, :]))
            for eng in ("vector", "tensor", "scalar", "gpsimd"):
                b.wait(eng, "DS", st)

        dbg_state = {"done": True}
        if dbg:
            dbg_state = {"done": False}
        for g in range(NG):
            group(xp, g, own=False)
            if dbg and g == 0:
                dbg_dump()
        b.wait("vector", "T", b.cur("T"))
        vop(lambda e: e.tensor_scalar(
            A_run[:], A_P[:], smask[:, 0:1], None, ALU.mult))
        vop(lambda e: e.tensor_scalar(
            B_run[:], B_P[:], smask[:, 0:1], None, ALU.mult))
        vmi = vop(lambda e: e.tensor_scalar(
            Ep_run[:], EP_P[:], smask[0:1, 0:1], None, ALU.mult))
        b.wait("vector", "V", vmi)
        vop(lambda e: e.tensor_copy(Abf[0][:], A_run[:]))
        vop(lambda e: e.tensor_copy(Bbf[0][:], B_run[:]))
        vop(lambda e: e.tensor_copy(Epbf[0][:], Ep_run[:]))
        for g in range(NG):
            group(xo, g, own=True)

        b.wait("sync", "DS", b.cur("DS"))
        b.wait("gpsimd", "DG", b.cur("DG"))

        with nc.Block() as block:
            @block.sync
            def _(eng):
                for fn in b.q["sync"]:
                    fn(eng)

            @block.gpsimd
            def _(eng):
                for fn in b.q["gpsimd"]:
                    fn(eng)

            @block.scalar
            def _(eng):
                for fn in b.q["scalar"]:
                    fn(eng)

            @block.vector
            def _(eng):
                for fn in b.q["vector"]:
                    fn(eng)

            @block.tensor
            def _(eng):
                for fn in b.q["tensor"]:
                    fn(eng)

    return nc


_NC_CACHE = {}


def _get_nc():
    if "nc" not in _NC_CACHE:
        _NC_CACHE["nc"] = build_program()
    return _NC_CACHE["nc"]


def kernel(L0_attn_hidden_states, L0_attn_Wq, L0_attn_Wk, L0_attn_Wv, L0_attn_Wg,
           L0_attn_Ws, L0_attn_Wo, L0_attn_g_norm_weight, L0_ffn_w_fc,
           L0_ffn_w_proj):
    x = np.ascontiguousarray(np.asarray(L0_attn_hidden_states, dtype=np.float32))
    consts = {
        "wq": np.ascontiguousarray(L0_attn_Wq, dtype=np.float32),
        "wk": np.ascontiguousarray(L0_attn_Wk, dtype=np.float32),
        "wv": np.ascontiguousarray(L0_attn_Wv, dtype=np.float32),
        "wg": np.ascontiguousarray(L0_attn_Wg, dtype=np.float32),
        "ws": np.ascontiguousarray(L0_attn_Ws, dtype=np.float32),
        "wo": np.ascontiguousarray(L0_attn_Wo, dtype=np.float32),
        "gnw": np.ascontiguousarray(
            np.asarray(L0_attn_g_norm_weight, dtype=np.float32).reshape(128, 1)),
        "wfc": np.ascontiguousarray(L0_ffn_w_fc, dtype=np.float32),
        "wpj": np.ascontiguousarray(L0_ffn_w_proj, dtype=np.float32),
        "ident": np.eye(128, dtype=bfd),
        "triu": np.triu(np.ones((128, 128), dtype=np.float32)).astype(bfd),
    }
    in_maps = []
    for c in range(8):
        a, h = c // 2, c % 2
        xo_ = np.ascontiguousarray(x[a, h * 2048:(h + 1) * 2048, :])
        xp_ = np.ascontiguousarray(x[a, 0:2048, :]) if h == 1 else xo_
        sm = np.full((128, 1), float(h), np.float32)
        m = dict(consts)
        m.update({"xo": xo_, "xp": xp_, "smask": sm})
        in_maps.append(m)
    nc = _get_nc()
    res = run_bass_kernel_spmd(nc, in_maps, core_ids=list(range(8)))
    outp = np.empty((4, 4096, 2048), np.float32)
    for c in range(8):
        a, h = c // 2, c % 2
        outp[a, h * 2048:(h + 1) * 2048, :] = res.results[c]["out"]
    return outp


# version 20
# speedup vs baseline: 1.0403x; 1.0403x over previous
import numpy as np
import ml_dtypes
import concourse.bass as bass
import concourse.mybir as mybir
from concourse.bass_utils import run_bass_kernel_spmd
from contextlib import ExitStack

F32 = mybir.dt.float32
BF16 = mybir.dt.bfloat16
AF = mybir.ActivationFunctionType
ALU = mybir.AluOpType
AX = mybir.AxisListType
bfd = ml_dtypes.bfloat16

D = 2048
TOK = 2048
NG = 4
H, DK, DV, M = 16, 64, 128, 64
FH = 5120
EPS = 1e-5
CLAMP = 32.0

UB_COLS = 36864
H2O = 0
Y2O = 0
QTO = 8192
KTO = 12288
EO = 16384
VO = 20480
GO = 28672


class Build:
    ENGINES = ("sync", "gpsimd", "scalar", "vector", "tensor")

    def __init__(self):
        self.q = {e: [] for e in self.ENGINES}
        self.ct = {}

    def add_sem(self, name, handle):
        self.ct[name] = [handle, 0]

    def op(self, eng, fn, inc=None):
        if inc is not None:
            cname, n = inc
            self.ct[cname][1] += n
            newv = self.ct[cname][1]
            h = self.ct[cname][0]

            def run(e, fn=fn, h=h, n=n):
                fn(e).then_inc(h, n)
            self.q[eng].append(run)
            return newv
        self.q[eng].append(lambda e, fn=fn: fn(e))
        return None

    def wait(self, eng, cname, val=None):
        h, n = self.ct[cname]
        v = n if val is None else val
        assert v <= n, f"wait on future: {eng} waits {cname}>={v} but emitted {n}"
        if v <= 0:
            return
        self.q[eng].append(lambda e, h=h, v=v: e.wait_ge(h, v))

    def cur(self, cname):
        return self.ct[cname][1]


def build_program(dbg=False):
    nc = bass.Bass("TRN2", target_bir_lowering=False, debug=False)

    ap = lambda t: t.ap()
    xo = ap(nc.dram_tensor("xo", [TOK, D], F32, kind="ExternalInput"))
    xp = ap(nc.dram_tensor("xp", [TOK, D], F32, kind="ExternalInput"))
    wq = ap(nc.dram_tensor("wq", [D, 1024], F32, kind="ExternalInput"))
    wk = ap(nc.dram_tensor("wk", [D, 1024], F32, kind="ExternalInput"))
    wv = ap(nc.dram_tensor("wv", [D, 2048], F32, kind="ExternalInput"))
    wg = ap(nc.dram_tensor("wg", [D, 2048], F32, kind="ExternalInput"))
    ws = ap(nc.dram_tensor("ws", [D, 1024], F32, kind="ExternalInput"))
    wo = ap(nc.dram_tensor("wo", [2048, D], F32, kind="ExternalInput"))
    gnw = ap(nc.dram_tensor("gnw", [128, 1], F32, kind="ExternalInput"))
    wfc = ap(nc.dram_tensor("wfc", [FH, D], F32, kind="ExternalInput"))
    wpj = ap(nc.dram_tensor("wpj", [D, FH], F32, kind="ExternalInput"))
    ident_in = ap(nc.dram_tensor("ident", [128, 128], BF16, kind="ExternalInput"))
    triu_in = ap(nc.dram_tensor("triu", [128, 128], BF16, kind="ExternalInput"))
    smask_in = ap(nc.dram_tensor("smask", [128, 1], F32, kind="ExternalInput"))
    out = ap(nc.dram_tensor("out", [TOK, D], F32, kind="ExternalOutput"))
    tfcT = ap(nc.dram_tensor("tfcT", [40, 128, 2048], BF16, kind="Internal"))
    tpj = ap(nc.dram_tensor("tpj", [D, FH], BF16, kind="Internal"))
    tpjT = ap(nc.dram_tensor("tpjT", [FH, D], BF16, kind="Internal"))
    if dbg:
        d_sc = ap(nc.dram_tensor("d_sc", [128, 8], F32, kind="ExternalOutput"))
        d_tfc = ap(nc.dram_tensor("d_tfc", [128, 2048], BF16, kind="ExternalOutput"))
        d_tpj = ap(nc.dram_tensor("d_tpj", [128, 2048], BF16, kind="ExternalOutput"))
        d_x1 = ap(nc.dram_tensor("d_x1", [128, 2048], F32, kind="ExternalOutput"))
        d_y1t = ap(nc.dram_tensor("d_y1t", [128, 2048], BF16, kind="ExternalOutput"))
        d_ln = ap(nc.dram_tensor("d_ln", [128, 2048], BF16, kind="ExternalOutput"))
        d_st = ap(nc.dram_tensor("d_st", [128, 8], F32, kind="ExternalOutput"))
        d_h2 = ap(nc.dram_tensor("d_h2", [128, 20480], BF16, kind="ExternalOutput"))

    b = Build()
    ctx = ExitStack()
    with ctx:
        sb = lambda name, shape, dt: ctx.enter_context(nc.sbuf_tensor(name, shape, dt))
        pst = lambda name, shape, dt: ctx.enter_context(nc.psum_tensor(name, shape, dt))

        UB = sb("UB", [128, UB_COLS], BF16)
        xg = [sb(f"xg{t}", [128, D], F32) for t in range(4)]
        wblk = [sb(f"wblk{i}", [128, 2048], BF16) for i in range(4)]
        relu_f = [sb(f"relu{i}", [128, 512], F32) for i in range(2)]
        A_P = sb("A_P", [128, 512], F32)
        B_P = sb("B_P", [128, 1024], F32)
        EP_P = sb("EP_P", [1, 1024], F32)
        A_run = sb("A_run", [128, 512], F32)
        B_run = sb("B_run", [128, 1024], F32)
        Ep_run = sb("Ep_run", [1, 1024], F32)
        Abf = [sb(f"Abf{i}", [128, 512], BF16) for i in range(2)]
        Bbf = [sb(f"Bbf{i}", [128, 1024], BF16) for i in range(2)]
        Epbf = [sb(f"Epbf{i}", [1, 1024], BF16) for i in range(2)]
        K_nat = sb("K_nat", [128, 1024], BF16)
        Et_b = sb("Et_b", [128, 1024], BF16)
        rT_b = sb("rT_b", [128, 1024], BF16)
        Ptm = [sb(f"Ptm{i}", [128, 128], BF16) for i in range(2)]
        REm = [sb(f"REm{i}", [128, 128], BF16) for i in range(2)]
        ok_sb = sb("ok_sb", [128, 1024], F32)
        Dr_sb = sb("Dr_sb", [128, 1024], F32)
        pe_bf = sb("pe_bf", [128, 1024], BF16)
        r_bf = sb("r_bf", [128, 1024], BF16)
        s_cl = sb("s_cl", [128, 512], F32)
        o_f = [sb(f"o_f{i}", [128, 512], F32) for i in range(2)]
        scrbf = sb("scrbf", [128, 2048], BF16)
        u_f = sb("u_f", [128, 512], F32)
        silu_bf = sb("silu_bf", [128, 512], BF16)
        sums = sb("sums", [128, 1], F32)
        sumsq = sb("sumsq", [128, 1], F32)
        m_t = sb("m_t", [128, 1], F32)
        msq = sb("msq", [128, 1], F32)
        ex2 = sb("ex2", [128, 1], F32)
        var_t = sb("var_t", [128, 1], F32)
        rstd = sb("rstd", [128, 1], F32)
        s16 = sb("s16", [128, 16], F32)
        rs16 = sb("rs16", [128, 16], F32)
        rstd16 = sb("rstd16", [128, 16], F32)
        ident = sb("identt", [128, 128], BF16)
        triu = sb("triut", [128, 128], BF16)
        onesc_bf = sb("onesc_bf", [128, 1], BF16)
        onesr_bf = sb("onesr_bf", [1, 128], BF16)
        onesc_f = sb("onesc_f", [128, 1], F32)
        onesr_f = sb("onesr_f", [1, 128], F32)
        onesq_f = sb("onesq_f", [128, 128], F32)
        smask = sb("smaskt", [128, 1], F32)
        gnormc = sb("gnormc", [128, 1], F32)
        asum = sb("asum", [128, 1], F32)
        asum_t = sb("asum_t", [128, 1], F32)
        asum_w = sb("asum_w", [128, 88], F32)
        am1 = sb("am1", [128, 1], F32)
        am1b = sb("am1b", [128, 1], F32)
        thr_fc = sb("thr_fc", [128, 1], F32)
        thr_pj = sb("thr_pj", [128, 1], F32)
        nthr_fc = sb("nthr_fc", [128, 1], F32)
        nthr_pj = sb("nthr_pj", [128, 1], F32)
        s_tot = sb("s_tot", [128, 1], F32)
        cvals = sb("cvals", [128, 8], F32)
        q2_bf = sb("q2_bf", [128, 2048], BF16)
        stg = sb("stgt", [128, 2048], BF16)
        twbuf = [xg[0], xg[1]]
        twbuf4 = {2: xg[2], 3: xg[3]}

        psD = pst("psD", [128, 1024], F32)
        psOK = pst("psOK", [128, 1024], F32)
        psOV = [pst(f"psOV{i}", [128, 512], F32) for i in range(2)]
        psTR = [pst(f"psTR{i}", [128, 512], BF16) for i in range(2)]
        psTile = [psD[:, 0:512], psD[:, 512:1024], psOK[:, 0:512], psOK[:, 512:1024]]

        semnames = ["T", "V", "S", "DS", "DG", "qst"]
        semnames += [f"qw{i}" for i in range(4)]
        semnames += [f"qs{i}" for i in range(16)]
        semnames += [f"qx{i}" for i in range(4)]
        semnames += [f"qb{i}" for i in range(4)]
        semnames += ["qcv0", "qcv1", "qxb0", "qxb1"]
        for name in semnames:
            b.add_sem(name, ctx.enter_context(nc.semaphore(name)))
        dma = 16

        _cvals = [EPS, -0.5, -1.0, 1.0 / 128.0, 1.0, 0.0]
        for _i, _v in enumerate(_cvals):
            nc.const_aps.register(F32, _v, cvals[:, _i:_i + 1])

        def vop(fn):
            return b.op("vector", fn, inc=("V", 1))

        def sop(fn):
            return b.op("scalar", fn, inc=("S", 1))

        def top(fn, inc=True):
            return b.op("tensor", fn, inc=("T", 1) if inc else None)

        def tnop():
            b.op("tensor", lambda e: e.nop() if hasattr(e, "nop")
                 else e.engine_nop())

        def dsop(fn, sem="DS"):
            return b.op("sync", fn, inc=(sem, dma))

        def dgop(fn, sem="DG"):
            return b.op("gpsimd", fn, inc=(sem, dma))

        ptr_free = {0: 0, 1: 0}

        def tr_batch(src_fn, n_in, emit_evac):
            bi = tr_batch.rot
            tr_batch.rot ^= 1
            pb = psTR[bi]
            b.wait("tensor", "V", ptr_free[bi])
            for j in range(n_in):
                top(lambda e, pb=pb, j=j: e.transpose(
                    pb[:, j * 128:(j + 1) * 128], src_fn(j), ident[:]),
                    inc=(j == n_in - 1))
            b.wait("vector", "T", b.cur("T"))
            ptr_free[bi] = emit_evac(pb, n_in)
        tr_batch.rot = 0

        def transpose_into(src_ap_fn, nchunks, dst_fn):
            for bb in range((nchunks + 3) // 4):
                n_in = min(4, nchunks - bb * 4)

                def evac(pb, n_in, bb=bb):
                    last = 0
                    for j in range(n_in):
                        ci = bb * 4 + j
                        last = vop(lambda e, pb=pb, j=j, ci=ci: e.tensor_copy(
                            dst_fn(ci), pb[:, j * 128:(j + 1) * 128]))
                    return last
                tr_batch(lambda j, bb=bb: src_ap_fn(bb * 4 + j), n_in, evac)

        dsop(lambda e: e.dma_start(out=ident[:], in_=ident_in[:]))
        dsop(lambda e: e.dma_start(out=triu[:], in_=triu_in[:]))
        dsop(lambda e: e.dma_start(out=smask[:], in_=smask_in[:]))
        dsop(lambda e: e.dma_start(out=gnormc[:], in_=gnw[:]))
        b.wait("vector", "DS", b.cur("DS"))
        vop(lambda e: e.memset(onesc_bf[:], 1.0))
        vop(lambda e: e.memset(onesr_bf[:], 1.0))
        vop(lambda e: e.memset(onesc_f[:], 1.0))
        vop(lambda e: e.memset(onesr_f[:], 1.0))
        vop(lambda e: e.memset(onesq_f[:], 1.0))
        vop(lambda e: e.memset(A_P[:], 0.0))
        vop(lambda e: e.memset(B_P[:], 0.0))
        vop(lambda e: e.memset(EP_P[:], 0.0))
        for _i, _v in enumerate([EPS, -0.5, -1.0, 1.0 / 128.0, 1.0, 0.0]):
            vop(lambda e, _i=_i, _v=_v: e.memset(cvals[:, _i:_i + 1], _v))

        def tern_chunks(nrows, cols):
            chunks = []
            for r in range(nrows):
                c0 = 0
                while c0 < cols:
                    cw = min(2048, cols - c0)
                    chunks.append((r, c0, cw))
                    c0 += cw
            return chunks

        def tern_stats(w_dram, nrows, cols, am_dst):
            nch = len(tern_chunks(nrows, cols))
            last_use = {i: b.cur("V") for i in range(4)}
            for i, (r, c0, cw) in enumerate(tern_chunks(nrows, cols)):
                bi = i % 4
                buf = xg[bi]
                b.wait("gpsimd", "V", last_use[bi])
                ld = dgop(lambda e, buf=buf, r=r, c0=c0, cw=cw: e.dma_start(
                    out=buf[:, 0:cw],
                    in_=w_dram[r * 128:(r + 1) * 128, c0:c0 + cw]),
                    sem=f"qx{bi}")
                b.wait("vector", f"qx{bi}", ld)
                last_use[bi] = vop(lambda e, buf=buf, cw=cw, i=i: e.reduce_sum(
                    asum_w[:, i:i + 1], buf[:, 0:cw], axis=AX.X,
                    apply_absolute_value=True))
            b.wait("vector", "V", b.cur("V"))
            vr = vop(lambda e, nch=nch: e.reduce_sum(
                asum_t[:], asum_w[:, 0:nch], axis=AX.X))
            b.wait("tensor", "V", vr)
            tm = top(lambda e: e.matmul(psOV[0][:, 0:1], onesq_f[:], asum_t[:],
                                        start=True, stop=True))
            b.wait("vector", "T", tm)
            n_el = nrows * 128 * cols
            vth = vop(lambda e, n_el=n_el, am_dst=am_dst: e.tensor_scalar(
                am_dst[:], psOV[0][:, 0:1], 1.0 / n_el, 1e-5, ALU.mult, ALU.max))
            b.wait("vector", "V", vth)

        pj_chunks = tern_chunks(16, FH)
        fc_free = {0: b.cur("V"), 1: b.cur("V")}
        pj_free = {2: b.cur("V"), 3: b.cur("V")}
        for i in range(len(pj_chunks)):
            if i < 40:
                bi = i % 2
                buf = xg[bi]
                b.wait("sync", "V", fc_free[bi])
                ld = dsop(lambda e, buf=buf, i=i: e.dma_start(
                    out=buf[:], in_=wfc[i * 128:(i + 1) * 128, :]),
                    sem=f"qx{bi}")
                b.wait("vector", f"qx{bi}", ld)
                fc_free[bi] = vop(lambda e, buf=buf, i=i: e.reduce_sum(
                    asum_w[:, i:i + 1], buf[:], axis=AX.X,
                    apply_absolute_value=True))
            r, c0, cw = pj_chunks[i]
            bi = 2 + i % 2
            buf = xg[bi]
            b.wait("gpsimd", "V", pj_free[bi])
            ld = dgop(lambda e, buf=buf, r=r, c0=c0, cw=cw: e.dma_start(
                out=buf[:, 0:cw], in_=wpj[r * 128:(r + 1) * 128, c0:c0 + cw]),
                sem=f"qx{bi}")
            b.wait("vector", f"qx{bi}", ld)
            pj_free[bi] = vop(lambda e, buf=buf, cw=cw, i=i: e.reduce_sum(
                asum_w[:, 40 + i:41 + i], buf[:, 0:cw], axis=AX.X,
                apply_absolute_value=True))
        b.wait("vector", "V", b.cur("V"))
        vr = vop(lambda e: e.reduce_sum(asum_t[:], asum_w[:, 0:40], axis=AX.X))
        vr2 = vop(lambda e, n=len(pj_chunks): e.reduce_sum(
            asum[:], asum_w[:, 40:40 + n], axis=AX.X))
        b.wait("tensor", "V", vr2)
        top(lambda e: e.matmul(psOV[0][:, 0:1], onesq_f[:], asum_t[:],
                               start=True, stop=True))
        tm = top(lambda e: e.matmul(psOV[0][:, 1:2], onesq_f[:], asum[:],
                                    start=True, stop=True))
        b.wait("vector", "T", tm)
        vop(lambda e: e.tensor_scalar(
            am1[:], psOV[0][:, 0:1], 1.0 / (40 * 128 * 2048), 1e-5,
            ALU.mult, ALU.max))
        vth2 = vop(lambda e: e.tensor_scalar(
            am1b[:], psOV[0][:, 1:2], 1.0 / (2048 * FH), 1e-5,
            ALU.mult, ALU.max))
        b.wait("vector", "V", vth2)
        vop(lambda e: e.tensor_scalar(thr_fc[:], am1[:], 0.7, None, ALU.mult))
        vop(lambda e: e.tensor_scalar(nthr_fc[:], am1[:], -0.7, None, ALU.mult))
        vop(lambda e: e.tensor_scalar(thr_pj[:], am1b[:], 0.7, None, ALU.mult))
        vop(lambda e: e.tensor_scalar(nthr_pj[:], am1b[:], -0.7, None, ALU.mult))
        va = vop(lambda e: e.tensor_tensor(s_tot[:], am1[:], am1[:], ALU.mult))
        b.wait("vector", "V", va)
        vb = vop(lambda e: e.tensor_tensor(s_tot[:], s_tot[:], am1b[:], ALU.mult))
        b.wait("vector", "V", vb)
        v2 = vop(lambda e: e.tensor_scalar(s_tot[:], s_tot[:], 0.5, None, ALU.mult))
        b.wait("vector", "V", v2)
        q_free_T = 0
        fc_free = {0: b.cur("V"), 1: b.cur("V")}
        pj_free = {2: b.cur("V"), 3: b.cur("V")}
        stg_free = {i: b.cur("qst") for i in range(4)}
        sreg = [0]
        for i in range(len(pj_chunks)):
            if i < 40:
                bi = i % 2
                buf = xg[bi]
                b.wait("sync", "V", fc_free[bi])
                ld = dsop(lambda e, buf=buf, i=i: e.dma_start(
                    out=buf[:], in_=wfc[i * 128:(i + 1) * 128, :]),
                    sem=f"qx{bi}")
                b.wait("vector", f"qx{bi}", ld)
                b.wait("vector", "T", q_free_T)
                vn = vop(lambda e, buf=buf: e.tensor_scalar(
                    scrbf[:], buf[:], nthr_fc[:, 0:1], None, ALU.is_lt))
                b.wait("vector", "V", vn)
                fc_free[bi] = vop(lambda e, buf=buf: e.scalar_tensor_tensor(
                    q2_bf[:], buf[:], thr_fc[:, 0:1], scrbf[:],
                    ALU.is_gt, ALU.subtract))
                b.wait("tensor", "V", fc_free[bi])
                for bb in range(4):
                    def evac(pb, n_in, bb=bb):
                        return vop(lambda e, pb=pb, bb=bb: e.tensor_copy(
                            stg[:, bb * 512:(bb + 1) * 512], pb[:]))
                    tr_batch(lambda j, bb=bb: q2_bf[:, (bb * 4 + j) * 128:
                                                    (bb * 4 + j + 1) * 128],
                             4, evac)
                q_free_T = b.cur("T")
                vs = b.cur("V")
                b.wait("sync", "V", vs)
                st = dsop(lambda e, i=i: e.dma_start(
                    out=tfcT[i, :, :], in_=stg[:]), sem="qst")
                b.wait("vector", "qst", st)
            r, c0, cw = pj_chunks[i]
            bi = 2 + i % 2
            buf = xg[bi]
            b.wait("gpsimd", "V", pj_free[bi])
            ld = dgop(lambda e, buf=buf, r=r, c0=c0, cw=cw: e.dma_start(
                out=buf[:, 0:cw], in_=wpj[r * 128:(r + 1) * 128, c0:c0 + cw]),
                sem=f"qx{bi}")
            b.wait("vector", f"qx{bi}", ld)
            vn = vop(lambda e, buf=buf, cw=cw: e.tensor_scalar(
                scrbf[:, 0:cw], buf[:, 0:cw], nthr_pj[:, 0:1], None, ALU.is_lt))
            b.wait("vector", "V", vn)
            pj_free[bi] = vop(lambda e, buf=buf, cw=cw: e.scalar_tensor_tensor(
                q2_bf[:, 0:cw], buf[:, 0:cw], thr_pj[:, 0:1], scrbf[:, 0:cw],
                ALU.is_gt, ALU.subtract))
            b.wait("tensor", "V", pj_free[bi])
            for bb in range(cw // 512):
                reg = sreg[0]
                sreg[0] = (sreg[0] + 1) % 4

                def evac(pb, n_in, reg=reg, r=r, c0=c0, bb=bb):
                    b.wait("vector", "qst", stg_free[reg])
                    last = 0
                    for j in range(4):
                        last = vop(lambda e, pb=pb, j=j, reg=reg: e.tensor_copy(
                            UB[:, reg * 512 + j * 128:reg * 512 + (j + 1) * 128],
                            pb[:, j * 128:(j + 1) * 128]))
                    b.wait("gpsimd", "V", last)
                    stg_free[reg] = dgop(
                        lambda e, reg=reg, r=r, c0=c0, bb=bb: e.dma_start(
                            out=tpjT[c0 + bb * 512:c0 + (bb + 1) * 512,
                                     r * 128:(r + 1) * 128].rearrange(
                                "(j p) c -> p j c", j=4),
                            in_=UB[:, reg * 512:(reg + 1) * 512].rearrange(
                                "p (j c) -> p j c", j=4)), sem="qst")
                    return last
                tr_batch(lambda j, bb=bb: q2_bf[:, bb * 512 + j * 128:
                                                bb * 512 + (j + 1) * 128],
                         4, evac)
        b.wait("sync", "qst", b.cur("qst"))
        b.wait("gpsimd", "qst", b.cur("qst"))
        b.wait("vector", "qst", b.cur("qst"))

        def y1T_ap(dc):
            return UB[:, VO + dc * 512:VO + (dc + 1) * 512]

        def y2T_ap(dc):
            return UB[:, Y2O + dc * 512:Y2O + (dc + 1) * 512]

        def y2T_tile_ap(dc, t):
            c = Y2O + dc * 512 + t * 128
            return UB[:, c:c + 128]

        def y3T_ap(dc):
            v = UB[:, GO:GO + 8192].rearrange("p (t c) -> p t c", t=4)
            return v[:, :, dc * 128:(dc + 1) * 128]

        def E_ap(t, h=None):
            c = EO + t * 1024
            if h is None:
                return UB[:, c:c + 1024]
            return UB[:, c + h * 64:c + (h + 1) * 64]

        def V_ap(t, h):
            c = VO + t * 2048 + h * 128
            return UB[:, c:c + 128]

        def G_ap(t, sl):
            c = GO + t * 2048 + sl * 512
            return UB[:, c:c + 512]

        def qt_ap(h, t):
            p0 = 64 * (h % 2)
            c = QTO + (h // 2) * 512 + t * 128
            return UB[p0:p0 + 64, c:c + 128]

        def kt_ap(h, t):
            p0 = 64 * (h % 2)
            c = KTO + (h // 2) * 512 + t * 128
            return UB[p0:p0 + 64, c:c + 128]

        def gT_ap(vc, t):
            off = QTO if vc < 8 else KTO
            c = off + (vc % 8) * 512 + t * 128
            return UB[:, c:c + 128]

        def a_blk(tn, h):
            p0 = 64 * (h % 2)
            c = (h // 2) * 64
            return tn[p0:p0 + 64, c:c + 64]

        def b_blk(tn, h):
            p0 = 64 * (h % 2)
            c = (h // 2) * 128
            return tn[p0:p0 + 64, c:c + 128]

        def ln_tile(x_t, dst_off, t, tmajor=False, dbg_tap=False):
            b.wait("vector", "S", b.cur("S"))
            vop(lambda e, x_t=x_t: e.reduce_sum(sums[:], x_t[:], axis=AX.X))
            vs = vop(lambda e, x_t=x_t: e.scalar_tensor_tensor(
                scrbf[:, 0:2048], x_t[:], 1.0, x_t[:], ALU.mult, ALU.mult,
                accum_out=sumsq[:]))
            b.wait("vector", "V", vs)
            vm = vop(lambda e: e.tensor_scalar(m_t[:], sums[:], 1.0 / D, None,
                                               ALU.mult))
            b.wait("vector", "V", vm)
            vop(lambda e: e.tensor_tensor(msq[:], m_t[:], m_t[:], ALU.mult))
            vx = vop(lambda e: e.tensor_scalar(ex2[:], sumsq[:], 1.0 / D, None,
                                               ALU.mult))
            b.wait("vector", "V", vx)
            vv = vop(lambda e: e.tensor_tensor(var_t[:], ex2[:], msq[:], ALU.subtract))
            b.wait("scalar", "V", vv)
            sl_ = sop(lambda e: e.activation(out=rstd[:], in_=var_t[:], func=AF.Ln,
                                             bias=EPS))
            b.wait("scalar", "S", sl_)
            sr = sop(lambda e: e.activation(out=rstd[:], in_=rstd[:], func=AF.Exp,
                                            scale=-0.5))
            b.wait("vector", "S", sr)
            vy = vop(lambda e, x_t=x_t: e.tensor_scalar(
                scrbf[:, 0:2048], x_t[:], m_t[:, 0:1], rstd[:, 0:1],
                ALU.subtract, ALU.mult))
            if dbg_tap:
                vop(lambda e: e.tensor_copy(s_cl[:, 0:1], sums[:]))
                vop(lambda e: e.tensor_copy(s_cl[:, 1:2], sumsq[:]))
                vop(lambda e: e.tensor_copy(s_cl[:, 2:3], m_t[:]))
                vop(lambda e: e.tensor_copy(s_cl[:, 3:4], msq[:]))
                vop(lambda e: e.tensor_copy(s_cl[:, 4:5], ex2[:]))
                vop(lambda e: e.tensor_copy(s_cl[:, 5:6], var_t[:]))
                vd = vop(lambda e: e.tensor_copy(s_cl[:, 6:7], rstd[:]))
                b.wait("sync", "V", vd)
                dsop(lambda e: e.dma_start(out=d_st[:], in_=s_cl[:, 0:8]))
                st_ = dsop(lambda e: e.dma_start(out=d_ln[:], in_=scrbf[:]))
                for eng in ("vector", "tensor", "scalar", "gpsimd"):
                    b.wait(eng, "DS", st_)
            b.wait("tensor", "V", vy)
            for bb in range(4):
                def evac(pb, n_in, bb=bb):
                    last = 0
                    for j in range(4):
                        dc = bb * 4 + j
                        if tmajor:
                            col = dst_off + t * 2048 + dc * 128
                        else:
                            col = dst_off + dc * 512 + t * 128
                        last = vop(lambda e, pb=pb, j=j, col=col: e.tensor_copy(
                            UB[:, col:col + 128], pb[:, j * 128:(j + 1) * 128]))
                    return last
                tr_batch(lambda j, bb=bb: scrbf[:, (bb * 4 + j) * 128:
                                                (bb * 4 + j + 1) * 128], 4, evac)

        def ffn(ysrc_ap_fn):
            b.wait("vector", "T", b.cur("T"))
            b.wait("vector", "S", b.cur("S"))
            b.wait("tensor", "V", b.cur("V"))
            b.wait("tensor", "S", b.cur("S"))
            wfree = {i: b.cur("T") for i in range(4)}
            sfree = {0: 0, 1: 0}
            relu_s = {0: 0, 1: 0}
            for fc in range(40):
                bi = fc % 4
                buf = wblk[bi]
                pi = fc % 2
                b.wait("sync", "T", wfree[bi])
                ld = dsop(lambda e, buf=buf, fc=fc: e.dma_start(
                    out=buf[:], in_=tfcT[fc, :, :]), sem=f"qw{bi}")
                pb = psOV[pi]
                b.wait("tensor", f"qw{bi}", ld)
                b.wait("tensor", "S", relu_s[pi])
                for dc in range(16):
                    top(lambda e, pb=pb, buf=buf, dc=dc: e.matmul(
                        pb[:], buf[:, dc * 128:(dc + 1) * 128], ysrc_ap_fn(dc),
                        start=(dc == 0), stop=(dc == 15)), inc=(dc == 15))
                wfree[bi] = b.cur("T")
                b.wait("scalar", "T", wfree[bi])
                b.wait("scalar", "V", sfree[pi])
                relu_s[pi] = sop(lambda e, pb=pb, pi=pi: e.activation(
                    out=relu_f[pi][:], in_=pb[:], func=AF.Relu))
                b.wait("vector", "S", relu_s[pi])
                sfree[pi] = vop(lambda e, pi=pi, fc=fc: e.scalar_tensor_tensor(
                    UB[:, H2O + fc * 512:H2O + (fc + 1) * 512],
                    relu_f[pi][:], 1.0, relu_f[pi][:], ALU.mult, ALU.mult))
            pw_free = {i: b.cur("T") for i in range(16)}
            for dg in range(4):
                b.wait("tensor", "V", b.cur("V"))
                for fc in range(40):
                    bi = (dg * 40 + fc) % 16
                    wsl = wblk[bi // 4][:, (bi % 4) * 512:(bi % 4 + 1) * 512]
                    b.wait("sync", "T", pw_free[bi])
                    ld = dsop(lambda e, wsl=wsl, dg=dg, fc=fc: e.dma_start(
                        out=wsl, in_=tpjT[fc * 128:(fc + 1) * 128,
                                          dg * 512:(dg + 1) * 512]),
                        sem=f"qs{bi}")
                    b.wait("tensor", f"qs{bi}", ld)
                    for t in range(4):
                        top(lambda e, t=t, wsl=wsl, fc=fc: e.matmul(
                            psTile[t],
                            UB[:, H2O + fc * 512 + t * 128:
                               H2O + fc * 512 + (t + 1) * 128],
                            wsl, start=(fc == 0), stop=(fc == 39)),
                            inc=(t == 3))
                    pw_free[bi] = b.cur("T")
                b.wait("vector", "T", b.cur("T"))
                for t in range(4):
                    vop(lambda e, t=t, dg=dg: e.scalar_tensor_tensor(
                        xg[t][:, dg * 512:(dg + 1) * 512], psTile[t],
                        s_tot[:, 0:1], xg[t][:, dg * 512:(dg + 1) * 512],
                        ALU.mult, ALU.add))

        def proj_transposed(w_dram, dst_off, scale):
            b.wait("tensor", "V", b.cur("V"))
            b.wait("tensor", "S", b.cur("S"))
            evac = {0: b.cur("V"), 1: b.cur("V")}
            for half in range(2):
                b.wait("gpsimd", "T", b.cur("T"))
                lds = []
                for dc in range(16):
                    lds.append(dgop(lambda e, dc=dc, half=half, w=w_dram: e.dma_start(
                        out=UB[:, VO + dc * 512:VO + (dc + 1) * 512],
                        in_=w[dc * 128:(dc + 1) * 128,
                              half * 512:(half + 1) * 512]),
                        sem=f"qb{dc // 4}"))
                for kci in range(4):
                    kc = half * 4 + kci
                    pb = psOV[kci % 2]
                    b.wait("tensor", "V", evac[kci % 2])
                    for dc in range(16):
                        if kci == 0 and dc % 4 == 0:
                            b.wait("tensor", f"qb{dc // 4}", lds[dc + 3])
                        top(lambda e, pb=pb, dc=dc, kci=kci: e.matmul(
                            pb[:],
                            UB[:, VO + dc * 512 + kci * 128:
                               VO + dc * 512 + (kci + 1) * 128],
                            y2T_ap(dc), start=(dc == 0), stop=(dc == 15)),
                            inc=(dc == 15))
                    b.wait("vector", "T", b.cur("T"))
                    evac[kci % 2] = vop(lambda e, pb=pb, kc=kc: e.tensor_scalar(
                        UB[:, dst_off + kc * 512:dst_off + (kc + 1) * 512],
                        pb[:], scale, None, ALU.mult))

        def proj_natural(w_dram, featdim, evac_fn):
            nvg = featdim // 512
            wfree = {i: b.cur("T") for i in range(16)}
            for vg in range(nvg):
                b.wait("tensor", "V", b.cur("V"))
                b.wait("tensor", "S", b.cur("S"))
                for dc in range(16):
                    bi = (vg * 16 + dc) % 16
                    wsl = wblk[bi // 4][:, (bi % 4) * 512:(bi % 4 + 1) * 512]
                    b.wait("gpsimd", "T", wfree[bi])
                    ld = dgop(lambda e, wsl=wsl, dc=dc, vg=vg, w=w_dram: e.dma_start(
                        out=wsl, in_=w[dc * 128:(dc + 1) * 128,
                                       vg * 512:(vg + 1) * 512]),
                        sem=f"qs{bi}")
                    b.wait("tensor", f"qs{bi}", ld)
                    for t in range(4):
                        top(lambda e, t=t, wsl=wsl, dc=dc: e.matmul(
                            psTile[t], y2T_tile_ap(dc, t), wsl,
                            start=(dc == 0), stop=(dc == 15)), inc=(t == 3))
                    wfree[bi] = b.cur("T")
                tl = b.cur("T")
                for t in range(4):
                    evac_fn(t, vg, tl)

        def state_updates(t, own, cidx):
            Arun = A_run if own else A_P
            Brun = B_run if own else B_P
            Eprun = Ep_run if own else EP_P
            b.wait("tensor", "V", b.cur("V"))
            for h in range(16):
                top(lambda e, h=h, t=t: e.matmul(
                    a_blk(psD, h), K_nat[:, h * 64:(h + 1) * 64], E_ap(t, h),
                    start=True, stop=True), inc=False)
            for h in range(16):
                top(lambda e, h=h, t=t: e.matmul(
                    b_blk(psOK, h), E_ap(t, h), V_ap(t, h),
                    start=True, stop=True), inc=False)
            top(lambda e, t=t: e.matmul(psOV[0][0:1, :], onesc_bf[:],
                                        E_ap(t)[:, 0:512], start=True, stop=True))
            top(lambda e, t=t: e.matmul(psOV[1][0:1, :], onesc_bf[:],
                                        E_ap(t)[:, 512:1024], start=True, stop=True),
                inc=True)
            b.wait("vector", "T", b.cur("T"))
            vop(lambda e, A=Arun: e.tensor_tensor(
                A[:], A[:], psD[:, 0:512], ALU.add))
            vop(lambda e, B=Brun: e.tensor_tensor(
                B[:], B[:], psOK[:], ALU.add))
            vop(lambda e, Ep=Eprun: e.tensor_tensor(
                Ep[:, 0:512], Ep[:, 0:512], psOV[0][0:1, :], ALU.add))
            vop(lambda e, Ep=Eprun: e.tensor_tensor(
                Ep[:, 512:1024], Ep[:, 512:1024], psOV[1][0:1, :], ALU.add))
            if own:
                b.wait("vector", "V", b.cur("V"))
                nb = (cidx + 1) % 2
                vop(lambda e, nb=nb: e.tensor_copy(Abf[nb][:], A_run[:]))
                vop(lambda e, nb=nb: e.tensor_copy(Bbf[nb][:], B_run[:]))
                vop(lambda e, nb=nb: e.tensor_copy(Epbf[nb][:], Ep_run[:]))

        def scan_own(g, t):
            cidx = 4 * g + t
            cb = cidx % 2
            transpose_into(lambda ci, t=t: UB[:, KTO + ci * 512 + t * 128:
                                              KTO + ci * 512 + t * 128 + 128],
                           8, lambda ci: K_nat[:, ci * 128:(ci + 1) * 128])
            transpose_into(lambda ci, t=t: UB[:, EO + t * 1024 + ci * 128:
                                              EO + t * 1024 + (ci + 1) * 128],
                           8, lambda ci: Et_b[:, ci * 128:(ci + 1) * 128])
            b.wait("tensor", "V", b.cur("V"))
            b.wait("tensor", "S", b.cur("S"))
            pt_order = list(range(0, 16, 2)) + list(range(1, 16, 2))
            for j, h in enumerate(pt_order):
                slot = psD[:, (h % 8) * 128:(h % 8 + 1) * 128] if h < 8 \
                    else psOK[:, (h % 8) * 128:(h % 8 + 1) * 128]
                if j == 8:
                    tnop()
                top(lambda e, slot=slot, h=h, t=t: e.matmul(
                    slot, kt_ap(h, t), qt_ap(h, t), start=True, stop=True),
                    inc=(j == 15))
            tptb = b.cur("T")
            b.wait("vector", "T", tptb)
            b.wait("vector", "S", b.cur("S"))
            for h in range(16):
                slot = psD[:, (h % 8) * 128:(h % 8 + 1) * 128] if h < 8 \
                    else psOK[:, (h % 8) * 128:(h % 8 + 1) * 128]
                vop(lambda e, slot=slot, h=h: e.tensor_tensor(
                    scrbf[:, h * 128:(h + 1) * 128], slot, triu[:], ALU.mult))
            vptm = b.cur("V")
            b.wait("tensor", "V", vptm)
            for sl in range(2):
                top(lambda e, sl=sl, t=t: e.matmul(
                    psD[:, sl * 512:(sl + 1) * 512], triu[:],
                    E_ap(t)[:, sl * 512:(sl + 1) * 512], start=True, stop=False))
                top(lambda e, sl=sl, cb=cb: e.matmul(
                    psD[:, sl * 512:(sl + 1) * 512], onesr_bf[:],
                    Epbf[cb][:, sl * 512:(sl + 1) * 512], start=False, stop=True))
            for j, h in enumerate(list(range(0, 16, 2)) + list(range(1, 16, 2))):
                top(lambda e, h=h, t=t: e.matmul(
                    psOK[:, h * 64:(h + 1) * 64],
                    scrbf[:, h * 128:(h + 1) * 128], E_ap(t, h),
                    start=True, stop=False))
                top(lambda e, h=h, t=t, cb=cb: e.matmul(
                    psOK[:, h * 64:(h + 1) * 64], qt_ap(h, t),
                    a_blk(Abf[cb], h), start=False, stop=True),
                    inc=(j == 15))
            b.wait("vector", "T", b.cur("T"))
            vrd = vop(lambda e: e.reciprocal(Dr_sb[:], psD[:]))
            b.wait("vector", "V", vrd)
            vok = vop(lambda e: e.tensor_tensor(ok_sb[:], psOK[:], Dr_sb[:], ALU.mult))
            b.wait("tensor", "V", vok)
            state_updates(t, True, cidx)
            b.wait("scalar", "V", vok)
            spe = sop(lambda e: e.activation(out=pe_bf[:], in_=ok_sb[:], func=AF.Exp))
            b.wait("vector", "S", spe)
            v16 = vop(lambda e: e.reduce_sum(
                s16[:], pe_bf[:].rearrange("p (h m) -> p h m", m=64), axis=AX.X))
            b.wait("vector", "V", v16)
            vr16 = vop(lambda e: e.reciprocal(rs16[:], s16[:]))
            b.wait("vector", "V", vr16)
            for h in range(16):
                vop(lambda e, h=h: e.scalar_tensor_tensor(
                    r_bf[:, h * 64:(h + 1) * 64], pe_bf[:, h * 64:(h + 1) * 64],
                    rs16[:, h:h + 1], Dr_sb[:, h * 64:(h + 1) * 64],
                    ALU.mult, ALU.mult))
            vrdone = b.cur("V")
            b.wait("tensor", "V", vrdone)
            transpose_into(lambda ci: r_bf[:, ci * 128:(ci + 1) * 128], 8,
                           lambda ci: rT_b[:, ci * 128:(ci + 1) * 128])
            b.wait("tensor", "V", b.cur("V"))
            for j, h in enumerate(list(range(0, 16, 2)) + list(range(1, 16, 2))):
                slot = psD[:, (h % 8) * 128:(h % 8 + 1) * 128] if h < 8 \
                    else psOK[:, (h % 8) * 128:(h % 8 + 1) * 128]
                if j == 8:
                    tnop()
                top(lambda e, slot=slot, h=h: e.matmul(
                    slot,
                    Et_b[64 * (h % 2):64 * (h % 2) + 64,
                         (h // 2) * 128:(h // 2 + 1) * 128],
                    rT_b[64 * (h % 2):64 * (h % 2) + 64,
                         (h // 2) * 128:(h // 2 + 1) * 128],
                    start=True, stop=True), inc=(j == 15))
            trtb = b.cur("T")
            b.wait("vector", "T", trtb)
            b.wait("vector", "S", b.cur("S"))
            for h in range(16):
                slot = psD[:, (h % 8) * 128:(h % 8 + 1) * 128] if h < 8 \
                    else psOK[:, (h % 8) * 128:(h % 8 + 1) * 128]
                vop(lambda e, slot=slot, h=h: e.tensor_tensor(
                    scrbf[:, h * 128:(h + 1) * 128], slot, triu[:], ALU.mult))
            vrem = b.cur("V")
            b.wait("tensor", "V", vrem)
            oev = {0: b.cur("V"), 1: b.cur("V")}
            for hp in range(2):
                heads = list(range(8 * hp, 8 * hp + 8))
                b.wait("tensor", "V", oev[0])
                b.wait("tensor", "V", oev[1])
                for j, h in enumerate([x for x in heads if x % 2 == 0]
                                      + [x for x in heads if x % 2 == 1]):
                    pov = psOV[(h // 4) % 2]
                    top(lambda e, pov=pov, h=h, t=t: e.matmul(
                        pov[:, (h % 4) * 128:(h % 4 + 1) * 128],
                        scrbf[:, h * 128:(h + 1) * 128],
                        V_ap(t, h), start=True, stop=False))
                    top(lambda e, pov=pov, h=h, cb=cb: e.matmul(
                        pov[:, (h % 4) * 128:(h % 4 + 1) * 128],
                        rT_b[64 * (h % 2):64 * (h % 2) + 64,
                             (h // 2) * 128:(h // 2 + 1) * 128],
                        b_blk(Bbf[cb], h), start=False, stop=True),
                        inc=(j == 7))
                tmark = b.cur("T")
                b.wait("vector", "T", tmark)
                for sl in (2 * hp, 2 * hp + 1):
                    pov_s = psOV[sl % 2]
                    ob = o_f[sl % 2]
                    oev[sl % 2] = vop(lambda e, ob=ob, pov_s=pov_s: e.tensor_copy(
                        ob[:], pov_s[:]))
                    b.wait("vector", "V", oev[sl % 2])
                    vsq = vop(lambda e, ob=ob, sl=sl: e.scalar_tensor_tensor(
                        scrbf[:, sl * 512:(sl + 1) * 512], ob[:], 1.0, ob[:],
                        ALU.mult, ALU.mult))
                    b.wait("vector", "V", vsq)
                    v2_ = vop(lambda e, sl=sl: e.reduce_sum(
                        s16[:, sl * 4:(sl + 1) * 4],
                        scrbf[:, sl * 512:(sl + 1) * 512].rearrange(
                            "p (hh v) -> p hh v", v=128), axis=AX.X))
                    b.wait("scalar", "V", v2_)
                    sln = sop(lambda e, sl=sl: e.activation(
                        out=rstd16[:, sl * 4:(sl + 1) * 4],
                        in_=s16[:, sl * 4:(sl + 1) * 4], func=AF.Ln,
                        scale=1.0 / 128.0, bias=EPS))
                    b.wait("scalar", "S", sln)
                    sr_ = sop(lambda e, sl=sl: e.activation(
                        out=rstd16[:, sl * 4:(sl + 1) * 4],
                        in_=rstd16[:, sl * 4:(sl + 1) * 4], func=AF.Exp,
                        scale=-0.5))
                    seg = sop(lambda e, t=t, sl=sl: e.activation(
                        out=scrbf[:, sl * 512:(sl + 1) * 512], in_=G_ap(t, sl),
                        func=AF.Exp, scale=-1.0))
                    b.wait("vector", "S", seg)
                    vu = vop(lambda e, sl=sl: e.tensor_scalar(
                        u_f[:], scrbf[:, sl * 512:(sl + 1) * 512], 1.0, None,
                        ALU.add))
                    b.wait("vector", "V", vu)
                    vrec = vop(lambda e: e.reciprocal(u_f[:], u_f[:]))
                    b.wait("vector", "V", vrec)
                    vsil = vop(lambda e, t=t, sl=sl: e.tensor_tensor(
                        silu_bf[:], G_ap(t, sl), u_f[:], ALU.mult))
                    b.wait("vector", "S", sr_)
                    b.wait("vector", "V", vsil)
                    for hh in range(4):
                        hx = sl * 4 + hh
                        vop(lambda e, ob=ob, hx=hx, hh=hh, sl=sl:
                            e.scalar_tensor_tensor(
                            scrbf[:, sl * 512 + hh * 128:
                                  sl * 512 + (hh + 1) * 128],
                            ob[:, hh * 128:(hh + 1) * 128], rstd16[:, hx:hx + 1],
                            silu_bf[:, hh * 128:(hh + 1) * 128],
                            ALU.mult, ALU.mult))
            vgate = b.cur("V")
            b.wait("tensor", "V", vgate)
            transpose_into(lambda ci: scrbf[:, ci * 128:(ci + 1) * 128], 16,
                           lambda ci, t=t: gT_ap(ci, t))

        def wo_phase(g):
            wfree = {i: b.cur("T") for i in range(16)}
            for dg in range(4):
                b.wait("tensor", "V", b.cur("V"))
                for vc in range(16):
                    bi = (dg * 16 + vc) % 16
                    wsl = wblk[bi // 4][:, (bi % 4) * 512:(bi % 4 + 1) * 512]
                    b.wait("gpsimd", "T", wfree[bi])
                    ld = dgop(lambda e, wsl=wsl, vc=vc, dg=dg: e.dma_start(
                        out=wsl, in_=wo[vc * 128:(vc + 1) * 128,
                                        dg * 512:(dg + 1) * 512]),
                        sem=f"qs{bi}")
                    b.wait("vector", f"qs{bi}", ld)
                    vf = vop(lambda e, wsl=wsl: e.tensor_scalar(
                        wsl, wsl, gnormc[:, 0:1], None, ALU.mult))
                    b.wait("tensor", "V", vf)
                    for t in range(4):
                        top(lambda e, t=t, wsl=wsl, vc=vc: e.matmul(
                            psTile[t], gT_ap(vc, t), wsl,
                            start=(vc == 0), stop=(vc == 15)), inc=(t == 3))
                    wfree[bi] = b.cur("T")
                b.wait("vector", "T", b.cur("T"))
                for t in range(4):
                    vop(lambda e, t=t, dg=dg: e.scalar_tensor_tensor(
                        xg[t][:, dg * 512:(dg + 1) * 512], psTile[t], 1.0,
                        xg[t][:, dg * 512:(dg + 1) * 512], ALU.mult, ALU.add))

        def group(src_dram, g, own):
            b.wait("sync", "V", b.cur("V"))
            b.wait("sync", "T", b.cur("T"))
            for t in range(4):
                ld = dsop(lambda e, t=t, g=g, src=src_dram: e.dma_start(
                    out=xg[t][:], in_=src[g * 512 + t * 128:
                                          g * 512 + (t + 1) * 128, :]),
                    sem=f"qx{t}")
                b.wait("vector", f"qx{t}", ld)
                tap = dbg and (not dbg_state["done"]) and t == 0 and g == 0
                ln_tile(xg[t], VO, t, dbg_tap=tap)
            if dbg and not dbg_state["done"]:
                vv = b.cur("V")
                b.wait("sync", "V", vv)
                st = dsop(lambda e: e.dma_start(out=d_y1t[:, 0:512],
                                                in_=UB[:, VO:VO + 512]))
                for eng in ("vector", "tensor", "scalar", "gpsimd"):
                    b.wait(eng, "DS", st)
            ffn(y1T_ap)
            if dbg and not dbg_state["done"]:
                dbg_state["done"] = True
                vv = b.cur("V")
                b.wait("sync", "V", vv)
                b.wait("sync", "T", b.cur("T"))
                dsop(lambda e: e.dma_start(out=d_x1[:], in_=xg[0][:]))
                st = dsop(lambda e: e.dma_start(out=d_h2[:],
                                                in_=UB[:, H2O:H2O + 20480]))
                for eng in ("vector", "tensor", "scalar", "gpsimd"):
                    b.wait(eng, "DS", st)
            for t in range(4):
                ln_tile(xg[t], Y2O, t)
            if own:
                proj_transposed(wq, QTO, DK ** -0.5)
            proj_transposed(wk, KTO, 1.0)
            s_exp = {"s": 0}

            def s_evac(t, vg, tmark):
                b.wait("vector", "T", tmark)
                b.wait("vector", "S", s_exp["s"])
                vc_ = vop(lambda e, t=t, vg=vg: e.tensor_scalar(
                    s_cl[:], psTile[t], CLAMP, -CLAMP, ALU.min, ALU.max))
                b.wait("scalar", "V", vc_)
                s_exp["s"] = sop(lambda e, t=t, vg=vg: e.activation(
                    out=E_ap(t)[:, vg * 512:(vg + 1) * 512], in_=s_cl[:],
                    func=AF.Exp))
            proj_natural(ws, 1024, s_evac)

            def v_evac(t, vg, tmark):
                b.wait("vector", "T", tmark)
                vop(lambda e, t=t, vg=vg: e.tensor_copy(
                    UB[:, VO + t * 2048 + vg * 512:VO + t * 2048 + (vg + 1) * 512],
                    psTile[t]))
            proj_natural(wv, 2048, v_evac)
            if own:
                def g_evac(t, vg, tmark):
                    b.wait("vector", "T", tmark)
                    vop(lambda e, t=t, vg=vg: e.tensor_copy(
                        G_ap(t, vg), psTile[t]))
                proj_natural(wg, 2048, g_evac)
            b.wait("tensor", "S", b.cur("S"))
            for t in range(4):
                if own:
                    scan_own(g, t)
                else:
                    transpose_into(
                        lambda ci, t=t: UB[:, KTO + ci * 512 + t * 128:
                                           KTO + ci * 512 + t * 128 + 128],
                        8, lambda ci: K_nat[:, ci * 128:(ci + 1) * 128])
                    state_updates(t, False, 0)
            if not own:
                return
            wo_phase(g)
            b.wait("vector", "S", b.cur("S"))
            for t in range(4):
                ln_tile(xg[t], GO, t, tmajor=True)
            ffn(y3T_ap)
            vend = b.cur("V")
            b.wait("sync", "V", vend)
            for t in range(4):
                dsop(lambda e, t=t, g=g: e.dma_start(
                    out=out[g * 512 + t * 128:g * 512 + (t + 1) * 128, :],
                    in_=xg[t][:]))

        def dbg_dump():
            vop(lambda e: e.memset(s_cl[:, 0:8], 0.0))
            vop(lambda e: e.tensor_copy(s_cl[:, 0:1], thr_fc[:]))
            vop(lambda e: e.tensor_copy(s_cl[:, 1:2], thr_pj[:]))
            vop(lambda e: e.tensor_copy(s_cl[:, 2:3], s_tot[:]))
            vop(lambda e: e.tensor_copy(s_cl[:, 3:4], am1[:]))
            vop(lambda e: e.tensor_copy(s_cl[:, 4:5], sums[:]))
            vop(lambda e: e.tensor_copy(s_cl[:, 5:6], sumsq[:]))
            vop(lambda e: e.tensor_copy(s_cl[:, 6:7], var_t[:]))
            vd = vop(lambda e: e.tensor_copy(s_cl[:, 7:8], rstd[:]))
            b.wait("sync", "V", vd)
            dsop(lambda e: e.dma_start(out=d_sc[:], in_=s_cl[:, 0:8]))
            dsop(lambda e: e.dma_start(out=d_tfc[:], in_=tfcT[39, :, :]))
            st = dsop(lambda e: e.dma_start(out=d_tpj[:], in_=tfcT[38, :, :]))
            for eng in ("vector", "tensor", "scalar", "gpsimd"):
                b.wait(eng, "DS", st)

        dbg_state = {"done": True}
        if dbg:
            dbg_state = {"done": False}
        for g in range(NG):
            group(xp, g, own=False)
            if dbg and g == 0:
                dbg_dump()
        b.wait("vector", "T", b.cur("T"))
        vop(lambda e: e.tensor_scalar(
            A_run[:], A_P[:], smask[:, 0:1], None, ALU.mult))
        vop(lambda e: e.tensor_scalar(
            B_run[:], B_P[:], smask[:, 0:1], None, ALU.mult))
        vmi = vop(lambda e: e.tensor_scalar(
            Ep_run[:], EP_P[:], smask[0:1, 0:1], None, ALU.mult))
        b.wait("vector", "V", vmi)
        vop(lambda e: e.tensor_copy(Abf[0][:], A_run[:]))
        vop(lambda e: e.tensor_copy(Bbf[0][:], B_run[:]))
        vop(lambda e: e.tensor_copy(Epbf[0][:], Ep_run[:]))
        for g in range(NG):
            group(xo, g, own=True)

        b.wait("sync", "DS", b.cur("DS"))
        b.wait("gpsimd", "DG", b.cur("DG"))

        with nc.Block() as block:
            @block.sync
            def _(eng):
                for fn in b.q["sync"]:
                    fn(eng)

            @block.gpsimd
            def _(eng):
                for fn in b.q["gpsimd"]:
                    fn(eng)

            @block.scalar
            def _(eng):
                for fn in b.q["scalar"]:
                    fn(eng)

            @block.vector
            def _(eng):
                for fn in b.q["vector"]:
                    fn(eng)

            @block.tensor
            def _(eng):
                for fn in b.q["tensor"]:
                    fn(eng)

    return nc


_NC_CACHE = {}


def _get_nc():
    if "nc" not in _NC_CACHE:
        _NC_CACHE["nc"] = build_program()
    return _NC_CACHE["nc"]


def kernel(L0_attn_hidden_states, L0_attn_Wq, L0_attn_Wk, L0_attn_Wv, L0_attn_Wg,
           L0_attn_Ws, L0_attn_Wo, L0_attn_g_norm_weight, L0_ffn_w_fc,
           L0_ffn_w_proj):
    x = np.ascontiguousarray(np.asarray(L0_attn_hidden_states, dtype=np.float32))
    consts = {
        "wq": np.ascontiguousarray(L0_attn_Wq, dtype=np.float32),
        "wk": np.ascontiguousarray(L0_attn_Wk, dtype=np.float32),
        "wv": np.ascontiguousarray(L0_attn_Wv, dtype=np.float32),
        "wg": np.ascontiguousarray(L0_attn_Wg, dtype=np.float32),
        "ws": np.ascontiguousarray(L0_attn_Ws, dtype=np.float32),
        "wo": np.ascontiguousarray(L0_attn_Wo, dtype=np.float32),
        "gnw": np.ascontiguousarray(
            np.asarray(L0_attn_g_norm_weight, dtype=np.float32).reshape(128, 1)),
        "wfc": np.ascontiguousarray(L0_ffn_w_fc, dtype=np.float32),
        "wpj": np.ascontiguousarray(L0_ffn_w_proj, dtype=np.float32),
        "ident": np.eye(128, dtype=bfd),
        "triu": np.triu(np.ones((128, 128), dtype=np.float32)).astype(bfd),
    }
    in_maps = []
    for c in range(8):
        a, h = c // 2, c % 2
        xo_ = np.ascontiguousarray(x[a, h * 2048:(h + 1) * 2048, :])
        xp_ = np.ascontiguousarray(x[a, 0:2048, :]) if h == 1 else xo_
        sm = np.full((128, 1), float(h), np.float32)
        m = dict(consts)
        m.update({"xo": xo_, "xp": xp_, "smask": sm})
        in_maps.append(m)
    nc = _get_nc()
    res = run_bass_kernel_spmd(nc, in_maps, core_ids=list(range(8)))
    outp = np.empty((4, 4096, 2048), np.float32)
    for c in range(8):
        a, h = c // 2, c % 2
        outp[a, h * 2048:(h + 1) * 2048, :] = res.results[c]["out"]
    return outp
